# version 08
# baseline (speedup 1.0000x reference)
import contextlib

import numpy as np
import ml_dtypes

import concourse.bass as bass
import concourse.mybir as mybir
from concourse.bass_utils import run_bass_kernel_spmd

F32 = mybir.dt.float32
BF16 = mybir.dt.bfloat16
I32 = mybir.dt.int32
AF = mybir.ActivationFunctionType
ALU = mybir.AluOpType
AX = mybir.AxisListType

D, T, DC = 1024, 4096, 8
NH = 8
H1, H1C = 4096, 32
HF, HFC = 3072, 24
TW = 256
NWIN = T // TW
NQ = 512
QMAX = 32767.0
EPS_LN, EPS_ATTN = 1e-5, 1e-6
NCORES = 8
SLAB = 1024
NSLAB = T // SLAB

ENGS = ("sync", "scalar", "vector", "tensor", "gpsimd")


def _flat(toks):
    out = []
    for t in toks:
        if t is None:
            continue
        if isinstance(t, list):
            out.extend(x for x in t if x is not None)
        else:
            out.append(t)
    return out


class Em:
    def __init__(self, nc, ctx, selfsync=False):
        self.selfsync = selfsync
        self.nc = nc
        self.ctx = ctx
        self.streams = {e: [] for e in ENGS}
        self.sems = {}
        self.cnt = {}
        self.bank_free = {}
        self.strm = {}
        for e in ENGS:
            self.new_sem("e_" + e)

    def new_sem(self, name):
        h = self.ctx.enter_context(self.nc.semaphore(name))
        self.sems[name] = h
        self.cnt[name] = 0
        return name

    def raw(self, eng, fn):
        self.streams[eng].append(fn)

    def waits(self, eng, toks):
        best = {}
        for n, v in _flat(toks):
            if n == "e_" + eng and not self.selfsync:
                continue
            best[n] = max(best.get(n, 0), v)
        for n, v in sorted(best.items()):
            h = self.sems[n]
            self.raw(eng, lambda E, h=h, v=v: E.wait_ge(h, v))

    def op(self, eng, fn, waits=(), inc=True, strict=False):
        if (self.selfsync or strict) and eng in ("vector", "scalar", "gpsimd"):
            n = "e_" + eng
            if self.cnt[n]:
                self.raw(eng, lambda E, h=self.sems[n], v=self.cnt[n]: E.wait_ge(h, v))
        self.waits(eng, waits)
        if inc:
            n = "e_" + eng
            self.cnt[n] += 1
            v = self.cnt[n]
            h = self.sems[n]
            self.raw(eng, lambda E, fn=fn, h=h: fn(E).then_inc(h, 1))
            return (n, v)
        self.raw(eng, lambda E, fn=fn: fn(E))
        return None

    def mark(self, eng):
        n = "e_" + eng
        self.cnt[n] += 1
        v = self.cnt[n]
        h = self.sems[n]
        last = self.streams[eng].pop()
        self.raw(eng, lambda E, last=last, h=h: last(E).then_inc(h, 1))
        return (n, v)

    def tcnt(self, eng):
        n = "e_" + eng
        return (n, self.cnt[n])

    def stream(self, name, q, nbuf=2):
        self.strm[name] = {"q": q, "sems": [self.new_sem(f"{name}_{i}") for i in range(nbuf)],
                           "idx": 0, "nbuf": nbuf}

    def stream_tok(self, name):
        st = self.strm[name]
        return [(s, self.cnt[s]) for s in st["sems"] if self.cnt[s]]

    def dma_group(self, sname, pairs, gate=()):
        st = self.strm[sname]
        i = st["idx"] % st["nbuf"]
        sem = st["sems"][i]
        prior = self.cnt[sem]
        g = _flat(gate) + ([(sem, prior)] if prior else [])
        self.waits(st["q"], g)
        h = self.sems[sem]
        for out, in_ in pairs:
            self.cnt[sem] += 16
            self.raw(st["q"], lambda E, out=out, in_=in_, h=h:
                     E.dma_start(out=out, in_=in_).then_inc(h, 16))
        st["idx"] += 1
        return (sem, self.cnt[sem])

    def bankw(self, key, waits=()):
        w = _flat(waits) + _flat([self.bank_free.get(key)])
        return w

    def bankr(self, key, tok):
        prev = self.bank_free.get(key)
        toks = _flat([prev, tok])
        self.bank_free[key] = toks

    def bankclr(self, key):
        self.bank_free[key] = None

    def finalize(self, block):
        for eng in ENGS:
            sec = getattr(block, eng)

            def body(E, eng=eng):
                for fn in self.streams[eng]:
                    fn(E)

            sec(body)


def build(taps=False, selfsync=False, phases=99):
    nc = bass.Bass("TRN2", target_bir_lowering=False,
                   detect_race_conditions=bool(selfsync))
    ctx = contextlib.ExitStack()

    def din(name, shape):
        return nc.dram_tensor(name, list(shape), F32, kind="ExternalInput").ap()

    xT = din("xT", (D, T))
    wqT = din("wqT", (D, D)); wkT = din("wkT", (D, D)); wvT = din("wvT", (D, D))
    w1T = din("w1T", (D, H1)); w2T = din("w2T", (H1, D))
    wuT = din("wuT", (D, HF)); wgT = din("wgT", (D, HF)); wdT = din("wdT", (HF, D))
    wmod = din("wmod", (D, 6 * D))
    cc_in = din("cc", (128, 8))
    vg1 = din("g1", (128, 8)); vbe1 = din("be1", (128, 8))
    vg2 = din("g2", (128, 8)); vbe2 = din("be2", (128, 8))
    vb1 = din("b1", (128, 32)); vb2 = din("b2", (128, 8))

    outT = nc.dram_tensor("outT", [D, T], F32, kind="ExternalOutput").ap()

    skind = "ExternalOutput" if taps else "Internal"
    h_s = nc.dram_tensor("h_s", [D, T], BF16, kind=skind).ap()
    qp_s = nc.dram_tensor("qp_s", [D, T], BF16, kind=skind).ap()
    xm_s = nc.dram_tensor("xm_s", [D, T], BF16, kind=skind).ap()
    pu_s = nc.dram_tensor("pu_s", [HF, T], BF16, kind=skind).ap()
    pg_s = nc.dram_tensor("pg_s", [HF, T], BF16, kind=skind).ap()
    t4_s = nc.dram_tensor("t4_s", [HF, T], BF16, kind=skind).ap()
    t5_s = nc.dram_tensor("t5_s", [HF, T], BF16, kind=skind).ap()
    dbg = nc.dram_tensor("dbg", [1, 64], F32, kind=skind).ap()
    xn_s = nc.dram_tensor("xn_s", [D, T], F32, kind=skind).ap()
    z_s = nc.dram_tensor("z_s", [NH, T], F32, kind=skind).ap()
    kc_s = nc.dram_tensor("kc_s", [128, 8], F32, kind=skind).ap()
    kr_s = nc.dram_tensor("kr_s", [128, 1024], BF16, kind=skind).ap()
    ac_s = nc.dram_tensor("ac_s", [D, T], F32, kind=skind).ap()
    bc_d = nc.dram_tensor("bc_d", [64, 128], F32, kind="Internal").ap()
    ar_in = [nc.dram_tensor(f"ar_in{i}", [2], F32, kind="Internal") for i in range(4)]
    ar_out = [nc.dram_tensor(f"ar_out{i}", [2], F32, kind="Internal") for i in range(4)]

    ones128_d = nc.inline_tensor(np.ones((128, 128), dtype=ml_dtypes.bfloat16), name="ones128_d")
    onesrow_d = nc.inline_tensor(np.ones((1, 128), dtype=np.float32), name="onesrow_d")
    ident1_d = nc.inline_tensor(np.ones((1, 1), dtype=np.float32), name="ident1_d")
    epsln_d = nc.inline_tensor(np.full((128, 1), EPS_LN, dtype=np.float32), name="epsln_d")
    epsattn_d = nc.inline_tensor(np.full((128, 1), EPS_ATTN, dtype=np.float32), name="epsattn_d")

    def sb(name, shape, dtype=F32):
        return ctx.enter_context(nc.sbuf_tensor(name, list(shape), dtype))

    def vsb(name, shape, dtype, backing):
        v = nc.virtual_sbuf_tensor(name, list(shape), dtype)
        nc.back_virtual_tensor_with(v, backing)
        return v

    with ctx:
        poolA = sb("poolA", (128, 32768), BF16)
        w1_bf = vsb("w1_bf", (128, 32768), BF16, poolA)
        h2_bf = vsb("h2_bf", (128, 32768), BF16, poolA)
        wd_bf = vsb("wd_bf", (128, 32768), BF16, poolA)
        poolC = sb("poolC", (128, 24576), BF16)
        bootC = vsb("bootC", (128, 24576), BF16, poolC)
        wqkv_bf = vsb("wqkv_bf", (128, 24576), BF16, poolC)
        w2a_bf = vsb("w2a_bf", (128, 24576), BF16, poolC)
        ffnC = vsb("ffnC", (128, 24576), BF16, poolC)
        w2b_bf = sb("w2b_bf", (128, 8192), BF16)

        xw = sb("xw", (128, 2 * DC * TW))
        sq_bf = sb("sq_bf", (128, DC * TW), BF16)
        xbf = sb("xbf", (128, DC * TW), BF16)
        tnorm = sb("tnorm", (128, DC * TW), BF16)
        stat = sb("stat", (128, 4 * TW))
        hw = sb("hw", (128, 2 * DC * TW), BF16)
        stg = vsb("stg", (128, 2 * DC * TW), BF16, hw)
        xbw = sb("xbw", (128, DC * TW), BF16)
        qpw = sb("qpw", (128, 2 * DC * TW), BF16)
        ipool = sb("ipool", (128, SLAB), I32)
        ktok = sb("ktok", (128, D), BF16)
        vtok = sb("vtok", (128, D), BF16)
        kv_bf = sb("kv_bf", (128, NH * 128), BF16)
        kcols = sb("kcols", (128, 8))
        ksrep_bf = sb("ksrep_bf", (128, NH * 128), BF16)
        zw = sb("zw", (128, TW))
        glt = sb("glt", (128, 2 * TW), BF16)
        e1t = sb("e1t", (128, 2 * NQ))
        escr = vsb("escr", (128, 2 * NQ), F32, e1t)
        ksr_bf = vsb("ksr_al", (1, D), F32, e1t)
        otile = sb("otile", (128, 2 * NQ))
        consts = sb("consts", (128, 64))
        cgb = sb("cgb", (128, 48))
        cvec = sb("cvec", (128, 80))
        ccs = sb("ccs", (128, 8))
        siluc_bf = sb("siluc_bf", (128, 8), BF16)
        ones128_bf = sb("ones128_bf", (128, 128), BF16)
        onesrow = sb("onesrow", (1, 128))
        ident1_bf = sb("ident1_bf", (1, 1))
        cells = sb("cells", (1, 32))
        epsln_col = sb("epsln_col", (128, 1))
        epsattn_col = sb("epsattn_col", (128, 1))
        rowt = sb("rowt", (1, 384))
        colt = sb("colt", (128, 16))
        rmax = sb("rmax", (128, 8))

        rem = nc.sbuf_bytes_remaining
        assert rem >= 0, f"SBUF over budget by {-rem} bytes"

        pbank = [ctx.enter_context(nc.psum_tensor(f"pb{i}", [128, NQ], F32)) for i in range(8)]
        PB = list(range(8))

        em = Em(nc, ctx, selfsync=selfsync)
        em.new_sem("ccdone")
        for name, q, nb in (("c0", "sync", 1), ("wm", "gpsimd", 2), ("wq3", "gpsimd", 1),
                            ("w1g", "gpsimd", 1), ("w2a", "gpsimd", 1), ("wd", "gpsimd", 1),
                            ("x", "sync", 2), ("hs", "gpsimd", 2), ("hr", "sync", 2),
                            ("qs", "gpsimd", 2), ("qr", "sync", 2), ("xs", "gpsimd", 2),
                            ("xr", "sync", 2), ("os", "gpsimd", 2), ("ff", "gpsimd", 2),
                            ("sg", "gpsimd", 4), ("pg", "sync", 4), ("t4w", "gpsimd", 2),
                            ("t4r", "sync", 4), ("pur", "sync", 4), ("t5w", "gpsimd", 2),
                            ("t5r", "sync", 2), ("fo", "gpsimd", 2), ("tz", "sync", 1),
                            ("bc", "gpsimd", 1), ("cc", "gpsimd", 1)):
            em.stream(name, q, nb)

        tok_consts = em.dma_group("c0", [
            (ccs[:, :], cc_in),
            (cvec[:, 0:8], vg1), (cvec[:, 8:16], vbe1), (cvec[:, 16:24], vg2),
            (cvec[:, 24:32], vbe2), (cvec[:, 32:40], vb2), (cvec[:, 40:72], vb1),
            (ones128_bf[:, :], ones128_d.ap()), (onesrow[:, :], onesrow_d.ap()),
            (ident1_bf[:, :], ident1_d.ap()), (epsln_col[:, :], epsln_d.ap()),
            (epsattn_col[:, :], epsattn_d.ap())])
        tk_cc = tk_vec = tk_misc = tk_eps = tok_consts
        COL_G1, COL_BE1, COL_G2, COL_BE2, COL_B2, COL_B1 = 0, 8, 16, 24, 32, 40

        def g1c(ic): return cvec[:, COL_G1 + ic:COL_G1 + ic + 1]
        def be1c(ic): return cvec[:, COL_BE1 + ic:COL_BE1 + ic + 1]
        def g2c(ic): return cvec[:, COL_G2 + ic:COL_G2 + ic + 1]
        def be2c(ic): return cvec[:, COL_BE2 + ic:COL_BE2 + ic + 1]
        def b1c(oc): return cvec[:, COL_B1 + oc:COL_B1 + oc + 1]

        t_silu = em.op("scalar", lambda E: E.activation(siluc_bf[:, :], ccs[:, :], AF.Silu),
                       waits=[tk_cc])
        mmtok = None
        prev_boot = None
        for ow in range(12):
            buf = ow % 2
            wslc = bootC[:, buf * 4096:(buf + 1) * 4096]
            gate = em.tcnt("tensor") if ow >= 2 else None
            ld = em.dma_group("wm", [
                (wslc[:, ic * 512:(ic + 1) * 512],
                 wmod[ic * 128:(ic + 1) * 128, ow * 512:(ow + 1) * 512])
                for ic in range(DC)], gate=[gate])
            for oc_rel in range(4):
                oc = ow * 4 + oc_rel
                for ic in range(DC):
                    em.op("tensor", lambda E, oc=oc, oc_rel=oc_rel, ic=ic, wslc=wslc: E.matmul(
                        pbank[0][:, oc:oc + 1],
                        wslc[:, ic * 512 + oc_rel * 128:ic * 512 + oc_rel * 128 + 128],
                        siluc_bf[:, ic:ic + 1],
                        start=(ic == 0), stop=(ic == DC - 1)),
                        waits=em.bankw(0, [ld, t_silu]) if (ic == 0 and oc_rel == 0) else [],
                        inc=False)
            mmtok = em.mark("tensor")
        t_modv = em.op("vector", lambda E: E.tensor_copy(consts[:, 0:48], pbank[0][:, 0:48]),
                       waits=[mmtok])
        em.bankr(0, t_modv)
        em.op("vector", lambda E: E.tensor_scalar(cgb[:, 0:8], consts[:, 8:16], 1.0, None, ALU.add),
              waits=[t_modv])
        em.op("vector", lambda E: E.tensor_copy(cgb[:, 8:16], consts[:, 0:8]))
        em.op("vector", lambda E: E.tensor_scalar(cgb[:, 16:24], consts[:, 32:40], 1.0, None, ALU.add))
        em.op("vector", lambda E: E.tensor_copy(cgb[:, 24:32], consts[:, 24:32]))

        t_quart = em.op("vector", lambda E: E.memset(colt[:, 7:8], 0.25))
        t_quart = em.op("vector", lambda E: E.memset(colt[:, 8:9], -0.25), waits=[t_quart])

        def clamp_gate(src_sl, dst_sl, tmp_sl, tmp2_sl, sgn_sl, prev):
            t0 = em.op("scalar", lambda E: E.activation(sgn_sl, src_sl, AF.Sign),
                       waits=[t_modv, prev])
            t1 = em.op("scalar", lambda E: E.activation(tmp_sl, src_sl, AF.Abs))
            t2 = em.op("scalar", lambda E: E.activation(tmp_sl, tmp_sl, AF.Relu,
                                                        bias=colt[:, 8:9]), waits=[t_quart])
            t3 = em.op("scalar", lambda E: E.activation(tmp2_sl, tmp_sl, AF.Identity,
                                                        bias=colt[:, 7:8]))
            return em.op("vector", lambda E: E.tensor_tensor(dst_sl, sgn_sl, tmp2_sl, ALU.mult),
                         waits=[t0, t3])

        t_ga = clamp_gate(consts[:, 16:24], cgb[:, 32:40], consts[:, 56:64], rmax[:, 0:8],
                          stat[:, 0:8], None)
        t_gb = clamp_gate(consts[:, 40:48], consts[:, 48:56], consts[:, 56:64], rmax[:, 0:8],
                          stat[:, 0:8], t_ga)
        GBP = consts[:, 48:56]
        t_gab2 = em.op("vector", lambda E: E.tensor_tensor(
            cgb[:, 40:48], cgb[:, 32:40], cvec[:, COL_B2:COL_B2 + 8], ALU.mult),
            waits=[tk_vec, t_gb])
        tok_mod_done = t_gab2

        def wpairs(dst, src_ap, icount, ocount):
            return [(dst[:, ic * ocount:(ic + 1) * ocount],
                     src_ap[ic * 128:(ic + 1) * 128, 0:ocount]) for ic in range(icount)]

        tok_wq3 = em.dma_group("wq3",
                               wpairs(wqkv_bf[:, 0:8192], wqT, DC, D) +
                               wpairs(wqkv_bf[:, 8192:16384], wkT, DC, D) +
                               wpairs(wqkv_bf[:, 16384:24576], wvT, DC, D), gate=[mmtok])
        tok_wq = tok_wk = tok_wv = tok_wq3
        tok_w1g = em.dma_group("w1g",
                               wpairs(w1_bf, w1T, DC, H1) +
                               wpairs(w2b_bf, w2T[24 * 128:32 * 128, 0:D], 8, D))
        tok_w1 = tok_w2b = tok_w1g

        def wq_blk(m, ic, oc):
            base = {0: 0, 1: 8192, 2: 16384}[m]
            return wqkv_bf[:, base + ic * D + oc * 128: base + ic * D + (oc + 1) * 128]

        state = {"sq_free": None, "xbf_free": None, "stats_free": None, "tnorm_free": None}

        def ln_stats(rhs_tile, sqsrc_ap, pre_toks):
            t_sq = em.op("scalar", lambda E: E.activation(sq_bf[:, :], sqsrc_ap, AF.Square),
                         waits=_flat(pre_toks) + _flat([state["sq_free"]]))
            for ic in range(DC):
                em.op("tensor", lambda E, ic=ic: E.matmul(
                    pbank[0][:, 0:TW], ones128_bf[:, :], rhs_tile[:, ic * TW:(ic + 1) * TW],
                    start=(ic == 0), stop=(ic == DC - 1)),
                    waits=em.bankw(0, pre_toks) if ic == 0 else [], inc=False)
            for ic in range(DC):
                em.op("tensor", lambda E, ic=ic: E.matmul(
                    pbank[1][:, 0:TW], ones128_bf[:, :], sq_bf[:, ic * TW:(ic + 1) * TW],
                    start=(ic == 0), stop=(ic == DC - 1)),
                    waits=em.bankw(1, [t_sq]) if ic == 0 else [], inc=False)
            t_mm = em.mark("tensor")
            state["sq_free"] = t_mm
            mu, r, mur, tmp = (stat[:, i * TW:(i + 1) * TW] for i in range(4))
            em.op("vector", lambda E: E.tensor_scalar(mu, pbank[0][:, 0:TW], 1.0 / D, None, ALU.mult),
                  waits=[t_mm])
            em.op("vector", lambda E: E.scalar_tensor_tensor(tmp, mu, 1.0, mu, ALU.bypass, ALU.mult))
            t_var = em.op("vector", lambda E: E.scalar_tensor_tensor(
                r, pbank[1][:, 0:TW], 1.0 / D, tmp, ALU.mult, ALU.subtract))
            em.bankr(0, t_var)
            em.bankr(1, t_var)
            t_sd = em.op("scalar", lambda E: E.activation(tmp, r, AF.Sqrt, bias=epsln_col[:, :]),
                         waits=[t_var])
            em.op("vector", lambda E: E.reciprocal(r, tmp), waits=[t_sd])
            t_st = em.op("vector", lambda E: E.tensor_tensor(mur, mu, r, ALU.mult))
            return t_st, r, mur

        def ln_apply(src_ap, r_sl, mur_sl, stat_tok, pre_toks=()):
            src3 = src_ap.rearrange("p (c t) -> p c t", c=DC)
            t3 = tnorm[:, :].rearrange("p (c t) -> p c t", c=DC)
            r3 = r_sl.unsqueeze(1).broadcast_to([128, DC, TW])
            m3 = mur_sl.unsqueeze(1).broadcast_to([128, DC, TW])
            em.op("vector", lambda E: E.scalar_tensor_tensor(t3, src3, 1.0, r3, ALU.bypass, ALU.mult),
                  waits=_flat([stat_tok]) + _flat(pre_toks) + _flat([state["tnorm_free"]]))
            return em.op("vector", lambda E: E.tensor_tensor(t3, t3, m3, ALU.subtract))

        def affine_out(dst_fn, scale_fn, bias_fn, t_norm, extra=()):
            toks = []
            for ic in range(DC):
                toks.append(em.op("scalar", lambda E, ic=ic: E.activation(
                    dst_fn(ic), tnorm[:, ic * TW:(ic + 1) * TW],
                    AF.Identity, scale=scale_fn(ic), bias=bias_fn(ic)),
                    waits=(_flat([t_norm]) + _flat(extra)) if ic == 0 else []))
            state["tnorm_free"] = toks[-1]
            return toks[-1]

        xw_free = [None, None]
        hw_write_gate = [None, None]
        qp_write_gate = [None, None]
        h_spill = None
        qp_spill = None
        prev_kv_mark = None

        for w in range(NWIN):
            buf = w % 2
            t0 = w * TW
            xsl = xw[:, buf * DC * TW:(buf + 1) * DC * TW]
            hsl = hw[:, buf * DC * TW:(buf + 1) * DC * TW]
            tok_x = em.dma_group("x", [
                (xsl[:, ic * TW:(ic + 1) * TW], xT[ic * 128:(ic + 1) * 128, t0:t0 + TW])
                for ic in range(DC)], gate=[xw_free[buf]])
            t_xbf = em.op("vector", lambda E, xsl=xsl: E.tensor_copy(xbf[:, :], xsl),
                          waits=[tok_x, tok_mod_done, state["xbf_free"]])
            st_tok, r_sl, mur_sl = ln_stats(xbf[:, :], xsl, [t_xbf, tok_x])
            state["xbf_free"] = state["sq_free"]
            t_n = ln_apply(xsl, r_sl, mur_sl, st_tok)
            t_h = affine_out(lambda ic, hsl=hsl: hsl[:, ic * TW:(ic + 1) * TW],
                             lambda ic: cgb[:, 0 + ic:1 + ic], lambda ic: cgb[:, 8 + ic:9 + ic],
                             t_n, extra=[hw_write_gate[buf]])
            xw_free[buf] = [t_n, t_h]
            h_spill = em.dma_group("hs", [
                (h_s[ic * 128:(ic + 1) * 128, t0:t0 + TW], hsl[:, ic * TW:(ic + 1) * TW])
                for ic in range(DC)], gate=[t_h])
            st2, r2, mur2 = ln_stats(hsl, hsl[:, :], [t_h])
            t_n2 = ln_apply(hsl, r2, mur2, st2)
            t_x2b = affine_out(lambda ic: xbw[:, ic * TW:(ic + 1) * TW], g1c, be1c, t_n2,
                               extra=[prev_kv_mark])

            qsl = qpw[:, buf * DC * TW:(buf + 1) * DC * TW]
            prev_q = qp_write_gate[buf]
            for oc in range(DC):
                bk = 2 + oc % 2
                for ic in range(DC):
                    em.op("tensor", lambda E, oc=oc, ic=ic, bk=bk: E.matmul(
                        pbank[bk][:, 0:TW], wq_blk(0, ic, oc), xbw[:, ic * TW:(ic + 1) * TW],
                        start=(ic == 0), stop=(ic == DC - 1)),
                        waits=em.bankw(bk, [t_x2b, tok_wq]) if ic == 0 else [], inc=False)
                t_q = em.mark("tensor")
                esl = escr[:, (oc % 4) * TW:(oc % 4 + 1) * TW]
                t_e = em.op("scalar", lambda E, bk=bk, esl=esl: E.activation(esl, pbank[bk][:, 0:TW],
                                                                             AF.Exp), waits=[t_q])
                em.op("vector", lambda E, esl=esl: E.tensor_scalar(esl, esl, 1.0, None, ALU.min),
                      waits=[t_e])
                prev_q = em.op("vector", lambda E, bk=bk, esl=esl, oc=oc, qsl=qsl:
                               E.scalar_tensor_tensor(qsl[:, oc * TW:(oc + 1) * TW], pbank[bk][:, 0:TW],
                                                      0.0, esl, ALU.max, ALU.add),
                               waits=[prev_q] if oc == 0 else [])
                em.bankr(bk, prev_q)
            qp_spill = em.dma_group("qs", [
                (qp_s[ic * 128:(ic + 1) * 128, t0:t0 + TW], qsl[:, ic * TW:(ic + 1) * TW])
                for ic in range(DC)], gate=[prev_q])
            qp_write_gate[buf] = qp_spill

            for sblk in range(TW // 128):
                tb0 = sblk * 128
                for nn in range(2):
                    bk = 2 + nn
                    for ic in range(DC):
                        em.op("tensor", lambda E, nn=nn, ic=ic, bk=bk, tb0=tb0: E.matmul(
                            pbank[bk][:, 0:NQ],
                            xbw[:, ic * TW + tb0:ic * TW + tb0 + 128],
                            wqkv_bf[:, 8192 + ic * D + nn * NQ:8192 + ic * D + (nn + 1) * NQ],
                            start=(ic == 0), stop=(ic == DC - 1)),
                            waits=em.bankw(bk, [t_x2b, tok_wk]) if ic == 0 else [], inc=False)
                t_k = em.mark("tensor")
                tk_done = []
                for nn in range(2):
                    bk = 2 + nn
                    esl = escr[:, nn * NQ:(nn + 1) * NQ]
                    t_e = em.op("scalar", lambda E, bk=bk, esl=esl: E.activation(
                        esl, pbank[bk][:, 0:NQ], AF.Exp), waits=[t_k])
                    em.op("vector", lambda E, esl=esl: E.tensor_scalar(esl, esl, 1.0, None, ALU.min),
                          waits=[t_e])
                    tk = em.op("vector", lambda E, bk=bk, esl=esl, nn=nn:
                               E.scalar_tensor_tensor(ktok[:, nn * NQ:(nn + 1) * NQ],
                                                      pbank[bk][:, 0:NQ], 0.0, esl, ALU.max, ALU.add))
                    em.bankr(bk, tk)
                    tk_done.append(tk)
                for hh in range(NH):
                    em.op("tensor", lambda E, hh=hh, w=w, sblk=sblk: E.matmul(
                        pbank[6][:, hh:hh + 1], ktok[:, hh * 128:(hh + 1) * 128],
                        ones128_bf[:, 0:1],
                        start=(w == 0 and sblk == 0 and hh == 0),
                        stop=(w == NWIN - 1 and sblk == TW // 128 - 1),
                        skip_group_check=True),
                        waits=[tk_done[hh // 4]] if hh in (0, 4) else [], inc=False)
                for nn in range(2):
                    bk = 2 + nn
                    for ic in range(DC):
                        em.op("tensor", lambda E, nn=nn, ic=ic, bk=bk, tb0=tb0: E.matmul(
                            pbank[bk][:, 0:NQ],
                            xbw[:, ic * TW + tb0:ic * TW + tb0 + 128],
                            wqkv_bf[:, 16384 + ic * D + nn * NQ:16384 + ic * D + (nn + 1) * NQ],
                            start=(ic == 0), stop=(ic == DC - 1)),
                            waits=em.bankw(bk, [tok_wv]) if ic == 0 else [], inc=False)
                t_v = em.mark("tensor")
                tv_done = []
                for nn in range(2):
                    bk = 2 + nn
                    tv = em.op("scalar", lambda E, bk=bk, nn=nn:
                               E.copy(vtok[:, nn * NQ:(nn + 1) * NQ], pbank[bk][:, 0:NQ]),
                               waits=[t_v])
                    em.bankr(bk, tv)
                    tv_done.append(tv)
                for hh in range(NH):
                    em.op("tensor", lambda E, hh=hh, w=w, sblk=sblk: E.matmul(
                        pbank[4 + hh // 4][:, (hh % 4) * 128:(hh % 4 + 1) * 128],
                        ktok[:, hh * 128:(hh + 1) * 128], vtok[:, hh * 128:(hh + 1) * 128],
                        start=(w == 0 and sblk == 0 and hh % 4 == 0),
                        stop=(w == NWIN - 1 and sblk == TW // 128 - 1),
                        skip_group_check=True),
                        waits=[tv_done[0], tv_done[1]] if hh == 0 else [], inc=False)
                prev_kv_mark = em.mark("tensor")
            hw_write_gate[buf] = [h_spill, t_n2, prev_kv_mark]

        kv_last = prev_kv_mark

        tok_w2a = em.dma_group("w2a", wpairs(w2a_bf, w2T, 24, D), gate=[kv_last])

        def w2_blk(ic, oc):
            if ic < 24:
                return w2a_bf[:, ic * D + oc * 128:ic * D + (oc + 1) * 128]
            return w2b_bf[:, (ic - 24) * D + oc * 128:(ic - 24) * D + (oc + 1) * 128]

        t_kv0 = em.op("vector", lambda E: E.tensor_copy(kv_bf[:, 0:NQ], pbank[4][:, 0:NQ]),
                      waits=[kv_last])
        t_kv = em.op("vector", lambda E: E.tensor_copy(kv_bf[:, NQ:2 * NQ], pbank[5][:, 0:NQ]))
        em.bankr(4, t_kv0); em.bankr(5, t_kv)
        t_kc = em.op("vector", lambda E: E.tensor_copy(kcols[:, 0:8], pbank[6][:, 0:8]),
                     waits=[kv_last])
        em.bankr(6, t_kc)
        if taps:
            em.dma_group("sg", [(kc_s, cgb[:, 32:40])], gate=[t_kc, tok_mod_done])
        t_krep = None
        for d in range(DC):
            t_krep = em.op("scalar", lambda E, d=d: E.activation(
                ksrep_bf[:, d * 128:(d + 1) * 128], ones128_bf[:, 0:128], AF.Identity,
                scale=kcols[:, d:d + 1]), waits=[t_kc] if d == 0 else [])
        if taps:
            em.dma_group("tz", [(kr_s, ksrep_bf[:, :])], gate=[t_krep])

        qp_rd_gate = [None, None]
        hw_rd_gate = [None, None]
        xm_spill = None
        tapz = None
        for w in range(NWIN):
            buf = w % 2
            t0 = w * TW
            qsl = qpw[:, buf * DC * TW:(buf + 1) * DC * TW]
            hsl = hw[:, buf * DC * TW:(buf + 1) * DC * TW]
            ldq = em.dma_group("qr", [
                (qsl[:, ic * TW:(ic + 1) * TW], qp_s[ic * 128:(ic + 1) * 128, t0:t0 + TW])
                for ic in range(DC)],
                gate=[qp_rd_gate[buf], em.stream_tok("qs")] + ([em.tcnt("tensor")] if w < 2 else []))
            ldh = em.dma_group("hr", [
                (hsl[:, ic * TW:(ic + 1) * TW], h_s[ic * 128:(ic + 1) * 128, t0:t0 + TW])
                for ic in range(DC)],
                gate=[hw_rd_gate[buf], em.stream_tok("hs")] + ([em.tcnt("tensor")] if w < 2 else []))
            den_mark = [None] * NH
            msg_mark = [None] * NH

            def emit_head_mms(hh, qsl=qsl, ldq=ldq):
                dbk = hh % 4
                em.op("tensor", lambda E, hh=hh, qsl=qsl, dbk=dbk: E.matmul(
                    pbank[dbk][:, 0:TW], ksrep_bf[:, hh * 128:(hh + 1) * 128],
                    qsl[:, hh * TW:(hh + 1) * TW], start=True, stop=True),
                    waits=em.bankw(dbk, [ldq, t_krep] if hh == 0 else []), inc=False)
                den_mark[hh] = em.mark("tensor")
                mbk = 4 + hh % 4
                em.op("tensor", lambda E, hh=hh, qsl=qsl, mbk=mbk: E.matmul(
                    pbank[mbk][:, 0:TW], kv_bf[:, hh * 128:(hh + 1) * 128],
                    qsl[:, hh * TW:(hh + 1) * TW], start=True, stop=True),
                    waits=em.bankw(mbk, []), inc=False)
                msg_mark[hh] = em.mark("tensor")

            for hh in range(4):
                emit_head_mms(hh)
            dn_free = [None, None]
            z_free = [None, None]
            ms_free = [None, None]
            t_xm = None
            for hh in range(NH):
                p = hh % 2
                dbk = hh % 4
                mbk = 4 + hh % 4
                dsl = stat[:, p * TW:(p + 1) * TW]
                zsl = zw[:, :] if p == 0 else otile[:, 0:TW]
                msl = stat[:, (2 + p) * TW:(3 + p) * TW]
                t_dn = em.op("scalar", lambda E, dsl=dsl, dbk=dbk: E.activation(
                    dsl, pbank[dbk][:, 0:TW], AF.Identity, bias=epsattn_col[:, :]),
                    waits=[den_mark[hh], dn_free[p]])
                em.bankr(dbk, t_dn)
                t_z = em.op("vector", lambda E, zsl=zsl, dsl=dsl: E.reciprocal(zsl, dsl),
                            waits=[t_dn, z_free[p], tapz])
                dn_free[p] = t_z
                t_ms = em.op("vector", lambda E, msl=msl, zsl=zsl, mbk=mbk:
                             E.scalar_tensor_tensor(msl, pbank[mbk][:, 0:TW],
                                                    1.0, zsl, ALU.bypass, ALU.mult),
                             waits=[msg_mark[hh], ms_free[p]] + ([ldh] if hh == 0 else []))
                em.bankr(mbk, t_ms)
                z_free[p] = t_ms
                t_xm = em.op("vector", lambda E, hh=hh, hsl=hsl, msl=msl: E.tensor_tensor(
                    hsl[:, hh * TW:(hh + 1) * TW], hsl[:, hh * TW:(hh + 1) * TW],
                    msl, ALU.add))
                ms_free[p] = t_xm
                if hh + 4 < NH:
                    emit_head_mms(hh + 4)
                if taps:
                    tapz = em.dma_group("tz", [(z_s[hh:hh + 1, t0:t0 + TW], zsl[0:1, :])],
                                        gate=[t_xm])
            qp_rd_gate[buf] = t_xm
            xm_spill = em.dma_group("xs", [
                (xm_s[ic * 128:(ic + 1) * 128, t0:t0 + TW], hsl[:, ic * TW:(ic + 1) * TW])
                for ic in range(DC)], gate=[t_xm])
            hw_rd_gate[buf] = xm_spill

        if phases <= 2:
            em.waits("sync", em.stream_tok("xs"))
            em.finalize(ctx.enter_context(nc.Block()))
            return nc
        out_spill = None
        xm_rd2 = [None, None]
        otile_gate = [None, None]
        e1_free = [None, None]
        for w in range(NWIN):
            buf = w % 2
            t0 = w * TW
            hsl = hw[:, buf * DC * TW:(buf + 1) * DC * TW]
            xsl = xw[:, buf * DC * TW:(buf + 1) * DC * TW]
            ldxm = em.dma_group("xr", [
                (hsl[:, ic * TW:(ic + 1) * TW], xm_s[ic * 128:(ic + 1) * 128, t0:t0 + TW])
                for ic in range(DC)], gate=[xm_rd2[buf], em.stream_tok("xs")])
            ldx2 = em.dma_group("x", [
                (xsl[:, ic * TW:(ic + 1) * TW], xT[ic * 128:(ic + 1) * 128, t0:t0 + TW])
                for ic in range(DC)], gate=[xw_free[buf]])
            st3, r3, mur3 = ln_stats(hsl, hsl[:, :], [ldxm])
            t_n3 = ln_apply(hsl, r3, mur3, st3)
            t_x2c = affine_out(lambda ic: xbw[:, ic * TW:(ic + 1) * TW], g2c, be2c, t_n3,
                               extra=[em.tcnt("tensor")])
            t_gl_done = [None, None]
            w1tok = [None] * H1C

            def emit_w1(oc):
                bk = 2 + oc % 2
                for ic in range(DC):
                    em.op("tensor", lambda E, oc=oc, ic=ic, bk=bk: E.matmul(
                        pbank[bk][:, 0:TW], w1_bf[:, ic * H1 + oc * 128:ic * H1 + (oc + 1) * 128],
                        xbw[:, ic * TW:(ic + 1) * TW], start=(ic == 0), stop=(ic == DC - 1)),
                        waits=em.bankw(bk, [t_x2c, tok_w1]) if ic == 0 else [], inc=False)
                w1tok[oc] = em.mark("tensor")

            emit_w1(0)
            emit_w1(1)
            for oc in range(H1C):
                bk = 2 + oc % 2
                gsl = qpw[:, (oc % 4) * TW:(oc % 4 + 1) * TW]
                t_gl = em.op("scalar", lambda E, bk=bk, gsl=gsl, oc=oc: E.activation(
                    gsl, pbank[bk][:, 0:TW], AF.Gelu, bias=b1c(oc)), waits=[w1tok[oc]])
                em.bankr(bk, t_gl)
                if oc + 2 < H1C:
                    emit_w1(oc + 2)
                if taps and w == 0 and oc < 4:
                    t_gl = em.dma_group("tz", [(kr_s[0:128, oc * TW:(oc + 1) * TW], gsl)],
                                        gate=[t_gl]) and t_gl
                    em.waits("scalar", em.stream_tok("tz"))
                for uc in range(DC):
                    abk = 4 + uc // 2
                    em.op("tensor", lambda E, uc=uc, oc=oc, gsl=gsl, abk=abk: E.matmul(
                        pbank[abk][:, (uc % 2) * TW:(uc % 2 + 1) * TW],
                        w2_blk(oc, uc), gsl, start=(oc == 0 and uc % 2 == 0),
                        stop=(oc == H1C - 1), skip_group_check=True),
                        waits=(em.bankw(4, []) + em.bankw(5, []) + em.bankw(6, []) +
                               em.bankw(7, []) + _flat([t_gl, tok_w2a, tok_w2b]))
                        if (uc == 0 and oc == 0) else ([t_gl] if uc == 0 else []),
                        inc=False)
                t_gl_done[oc % 2] = em.mark("tensor")
            t_acc = t_gl_done[(H1C - 1) % 2]
            for bk in (4, 5, 6, 7):
                em.bankclr(bk)
            t_out = None
            for uc in range(DC):
                abk = 4 + uc // 2
                esl = e1t[:, (uc % 2) * NQ:(uc % 2) * NQ + TW]
                t_e1 = em.op("scalar", lambda E, uc=uc, esl=esl, abk=abk: E.activation(
                    esl, pbank[abk][:, (uc % 2) * TW:(uc % 2 + 1) * TW], AF.Identity,
                    scale=cgb[:, 32 + uc:33 + uc], bias=cgb[:, 40 + uc:41 + uc]),
                    waits=([t_acc, tok_mod_done] if uc == 0 else []) + [e1_free[uc % 2]])
                em.bankr(abk, t_e1)
                if taps:
                    t_tap = em.dma_group("tz", [(ac_s[uc * 128:(uc + 1) * 128, t0:t0 + TW], esl)],
                                         gate=[t_e1])
                    t_e1 = [t_e1, t_tap]
                t_s2 = em.op("vector", lambda E, uc=uc, esl=esl, hsl=hsl: E.scalar_tensor_tensor(
                    esl, hsl[:, uc * TW:(uc + 1) * TW], cgb[:, 32 + uc:33 + uc], esl,
                    ALU.mult, ALU.add), waits=[t_e1])
                t_out = em.op("vector", lambda E, uc=uc, esl=esl, xsl=xsl: E.tensor_tensor(
                    otile[:, (uc % 2) * TW:(uc % 2 + 1) * TW],
                    xsl[:, uc * TW:(uc + 1) * TW], esl, ALU.add),
                    waits=[ldx2, otile_gate[uc % 2]])
                em.bankr(abk, t_s2)
                e1_free[uc % 2] = t_out
                out_spill = em.dma_group("os", [
                    (outT[uc * 128:(uc + 1) * 128, t0:t0 + TW],
                     otile[:, (uc % 2) * TW:(uc % 2 + 1) * TW])] + ([
                    (xn_s[uc * 128:(uc + 1) * 128, t0:t0 + TW],
                     otile[:, (uc % 2) * TW:(uc % 2 + 1) * TW])] if taps else []),
                    gate=[t_out])
                otile_gate[uc % 2] = out_spill
            xm_rd2[buf] = t_out
            xw_free[buf] = t_out

        if phases <= 3:
            em.waits("sync", em.stream_tok("os") + em.stream_tok("xr") + em.stream_tok("x") +
                     [em.tcnt("tensor"), em.tcnt("vector"), em.tcnt("scalar")])
            em.finalize(ctx.enter_context(nc.Block()))
            return nc
        n_bc = [0]
        xcm = {"rowt": None, "cells24": None, "rowbc": None}
        em.op("vector", lambda E: E.memset(cells[0:1, 0:32], 0.0))
        em.op("vector", lambda E: E.memset(cells[0:1, 15:16], 1e-12), strict=True)
        t_rz = em.op("vector", lambda E: E.memset(rmax[:, 0:8], 0.0))

        def cell(i):
            return cells[0:1, i:i + 1]

        def cross_core_max(col_sl, k, ncell, cell_lo, ready):
            toks = []
            for j in range(ncell):
                i = n_bc[0]; n_bc[0] += 1
                tk1 = em.dma_group("bc", [(bc_d[i:i + 1, 0:128].flatten(), col_sl[:, j:j + 1])],
                                   gate=[ready])
                toks.append(em.dma_group("bc", [(rowt[0:1, j * 128:(j + 1) * 128],
                                                 bc_d[i:i + 1, 0:128])],
                                         gate=[tk1, xcm["rowt"]]))
            t_loc = None
            for j in range(ncell):
                t_loc = em.op("vector", lambda E, j=j: E.tensor_reduce(
                    cells[0:1, 24 + j:25 + j], rowt[0:1, j * 128:(j + 1) * 128], AX.X, ALU.max),
                    waits=[toks[j], xcm["cells24"]])
            if ncell == 1:
                t_loc = em.op("vector", lambda E: E.tensor_copy(cells[0:1, 25:26],
                                                                cells[0:1, 24:25]),
                              strict=True)
            xcm["rowt"] = t_loc
            tin = em.dma_group("cc", [(ar_in[k].ap(), cells[0:1, 24:26])], gate=[t_loc])
            xcm["cells24"] = tin
            em.cnt["ccdone"] += 1
            t_cc = ("ccdone", em.cnt["ccdone"])
            em.op("gpsimd", lambda E, k=k: E.collective_compute(
                "AllReduce", ALU.max, replica_groups=[list(range(NCORES))],
                ins=[ar_in[k].ap().opt()], outs=[ar_out[k].ap().opt()])
                .then_inc(em.sems["ccdone"], 1), waits=[tin], inc=False)
            return em.dma_group("cc", [(cells[0:1, cell_lo:cell_lo + ncell],
                                        ar_out[k].ap()[0:ncell])], gate=[t_cc])

        def bcast_col(cell_sl, col_dst, ready):
            i = n_bc[0]; n_bc[0] += 1
            t_r = em.op("vector", lambda E: E.tensor_scalar(rowt[0:1, 256:384], onesrow[0:1, :],
                                                            cell_sl, None, ALU.mult),
                        waits=[ready, xcm["rowbc"]], strict=True)
            t1 = em.dma_group("bc", [(bc_d[i:i + 1, 0:128], rowt[0:1, 256:384])], gate=[t_r])
            xcm["rowbc"] = t1
            return em.dma_group("bc", [(col_dst, bc_d[i:i + 1, 0:128].flatten())], gate=[t1])

        def s_mul(dst, a, b, waits=()):
            return em.op("scalar", lambda E: E.mul(dst, a, b), waits=waits, strict=True)

        def s_maxeps(dst, a, waits=()):
            return em.op("vector", lambda E: E.tensor_tensor(dst, a, cell(15), ALU.max),
                         waits=waits, strict=True)

        def s_recip(dst, a, waits=()):
            return em.op("vector", lambda E: E.reciprocal(dst, a), waits=waits, strict=True)

        t_h2_last = None
        t_rm = t_rz
        for w in range(NWIN):
            buf = w % 2
            t0 = w * TW
            xsl = xw[:, buf * DC * TW:(buf + 1) * DC * TW]
            ldo = em.dma_group("x", [
                (xsl[:, ic * TW:(ic + 1) * TW], outT[ic * 128:(ic + 1) * 128, t0:t0 + TW])
                for ic in range(DC)], gate=[xw_free[buf], em.stream_tok("os")])
            t_xbf = em.op("vector", lambda E, xsl=xsl: E.tensor_copy(xbf[:, :], xsl),
                          waits=[ldo, state["xbf_free"]])
            state["xbf_free"] = None
            st4, r4, mur4 = ln_stats(xbf[:, :], xsl, [t_xbf, ldo])
            state["xbf_free"] = state["sq_free"]
            t_n4 = ln_apply(xsl, r4, mur4, st4)
            t_h2_last = affine_out(
                lambda ic, t0=t0: h2_bf[:, ic * T + t0:ic * T + t0 + TW],
                lambda ic: cgb[:, 16 + ic:17 + ic], lambda ic: cgb[:, 24 + ic:25 + ic],
                t_n4, extra=[em.tcnt("tensor") if w == 0 else None])
            xw_free[buf] = [t_n4, t_h2_last]
            h3 = h2_bf[:, 0:DC * T].rearrange("p (c t) -> p c t", c=DC)[:, :, t0:t0 + TW]
            em.op("vector", lambda E, h3=h3: E.tensor_reduce(
                rmax[:, 1:2], h3, AX.XY, ALU.max, apply_absolute_value=True),
                waits=[t_h2_last, t_rm])
            t_rm = em.op("vector", lambda E: E.tensor_tensor(rmax[:, 0:1], rmax[:, 0:1],
                                                             rmax[:, 1:2], ALU.max))

        t_m1 = cross_core_max(rmax[:, 0:1], 0, 1, 0, t_rm)
        t_s = s_mul(cell(16), cell(0), 1.0 / QMAX, waits=[t_m1])
        t_s1 = s_maxeps(cell(1), cell(16), waits=[t_s])
        t_is1 = s_recip(cell(2), cell(1), waits=[t_s1])
        t_is1c = bcast_col(cell(2), colt[:, 0:1], t_is1)

        tok_xq = None
        first_q = [t_is1c, t_h2_last, em.stream_tok("os")]
        qt_free = [None, None]
        sq = 0
        for ic in range(DC):
            for half in range(T // SLAB):
                p = sq % 2
                sq += 1
                tmp = otile[:, 0:SLAB] if p == 0 else xw[:, 0:SLAB]
                sl = h2_bf[:, ic * T + half * SLAB:ic * T + (half + 1) * SLAB]
                isl = ipool[:, 0:SLAB]
                t_a = None
                for c in range(SLAB // 256):
                    t_a = em.op("scalar", lambda E, sl=sl, c=c, tmp=tmp: E.activation(
                        tmp[:, c * 256:(c + 1) * 256], sl[:, c * 256:(c + 1) * 256],
                        AF.Identity, scale=colt[:, 0:1]),
                        waits=(first_q + [qt_free[p]]) if c == 0 else [])
                if sq == 2:
                    first_q = []
                t_i = em.op("vector", lambda E, isl=isl, tmp=tmp: E.tensor_copy(
                    isl, tmp), waits=[t_a])
                qt_free[p] = t_i
                tok_xq = em.op("vector", lambda E, sl=sl, isl=isl: E.tensor_copy(sl, isl),
                               waits=[t_i])

        if phases <= 4:
            em.waits("vector", [tok_xq])
            em.finalize(ctx.enter_context(nc.Block()))
            return nc
        stg_free = [None, None, None, None]
        stgi = [0]
        t_rmug = None
        for im, (wsrc, dst_s, rm_col) in enumerate(((wuT, pu_s, 3), (wgT, pg_s, 4))):
            for ow in range(HF // NQ):
                wbuf = ow % 2
                wsl = ffnC[:, wbuf * DC * NQ:(wbuf + 1) * DC * NQ]
                ldw = em.dma_group("ff", [
                    (wsl[:, ic * NQ:(ic + 1) * NQ],
                     wsrc[ic * 128:(ic + 1) * 128, ow * NQ:(ow + 1) * NQ])
                    for ic in range(DC)], gate=[em.tcnt("tensor"), tok_w2a])
                for oc_rel in range(4):
                    oc = ow * 4 + oc_rel
                    for tb in range(T // NQ):
                        bk = 2 + (oc_rel * 8 + tb) % 2
                        sidx = stgi[0] % 4
                        for ic in range(DC):
                            em.op("tensor", lambda E, bk=bk, wsl=wsl, oc_rel=oc_rel, ic=ic, tb=tb:
                                  E.matmul(pbank[bk][:, 0:NQ],
                                           wsl[:, ic * NQ + oc_rel * 128:ic * NQ + oc_rel * 128 + 128],
                                           h2_bf[:, ic * T + tb * NQ:ic * T + (tb + 1) * NQ],
                                           start=(ic == 0), stop=(ic == DC - 1)),
                                  waits=em.bankw(bk, [ldw, tok_xq, stg_free[sidx]])
                                  if ic == 0 else [], inc=False)
                        t_mm = em.mark("tensor")
                        ssl = stg[:, sidx * NQ:(sidx + 1) * NQ]
                        t_red = em.op("vector", lambda E, bk=bk: E.tensor_reduce(
                            rmax[:, 2:3], pbank[bk][:, 0:NQ], AX.X, ALU.max,
                            apply_absolute_value=True), waits=[t_mm])
                        t_ev = em.op("vector", lambda E, bk=bk, ssl=ssl: E.tensor_copy(
                            ssl, pbank[bk][:, 0:NQ]))
                        em.bankr(bk, t_ev)
                        t_rmug = em.op("vector", lambda E, rm_col=rm_col: E.tensor_tensor(
                            rmax[:, rm_col:rm_col + 1], rmax[:, rm_col:rm_col + 1],
                            rmax[:, 2:3], ALU.max))
                        stg_free[sidx] = em.dma_group("sg", [
                            (dst_s[oc * 128:(oc + 1) * 128, tb * NQ:(tb + 1) * NQ], ssl)],
                            gate=[t_ev])
                        stgi[0] += 1
        tok_wd = em.dma_group("wd", wpairs(wd_bf, wdT, HFC, D), gate=[em.tcnt("tensor")])

        em.op("vector", lambda E: E.tensor_copy(rmax[:, 6:7], rmax[:, 3:4]), waits=[t_rmug])
        t_mm2 = em.op("vector", lambda E: E.tensor_copy(rmax[:, 7:8], rmax[:, 4:5]))
        t_ex2 = cross_core_max(rmax[:, 6:8], 1, 2, 3, t_mm2)
        t_a = s_mul(cell(16), cell(3), cell(1), waits=[t_ex2, t_s1])
        t_a = s_mul(cell(16), cell(16), 1.0 / QMAX, waits=[t_a])
        t_s2c = s_maxeps(cell(5), cell(16), waits=[t_a])
        t_b = s_mul(cell(17), cell(4), cell(1), waits=[t_ex2, t_s1])
        t_b = s_mul(cell(17), cell(17), 1.0 / QMAX, waits=[t_b])
        t_s3c = s_maxeps(cell(6), cell(17), waits=[t_b])
        t_r2 = s_recip(cell(18), cell(5), waits=[t_s2c])
        t_r3 = s_recip(cell(19), cell(6), waits=[t_s3c])
        t_cg = s_mul(cell(7), cell(1), cell(19), waits=[t_r3])
        t_cu = s_mul(cell(8), cell(1), cell(18), waits=[t_r2])
        tc_cg = bcast_col(cell(7), colt[:, 1:2], t_cg)
        tc_cu = bcast_col(cell(8), colt[:, 2:3], t_cu)
        tc_s3 = bcast_col(cell(6), colt[:, 3:4], t_s3c)

        if phases <= 5:
            em.waits("sync", em.stream_tok("sg"))
            em.waits("vector", [tc_s3])
            em.finalize(ctx.enter_context(nc.Block()))
            return nc
        in_free = [None, None, None, None]
        out_free = [None, None]
        fb_free = [None, None]
        fb2_free = [None, None]
        t4_store = [None] * (HFC * NSLAB)
        t_rm4 = None
        for s in range(HFC * NSLAB):
            oc, half = divmod(s, NSLAB)
            b = s % 2
            b4 = s % 4
            insl = ffnC[:, 8192 + b4 * SLAB:8192 + (b4 + 1) * SLAB]
            outsl = ffnC[:, 16384 + b * SLAB:16384 + (b + 1) * SLAB]
            ld = em.dma_group("pg", [
                (insl, pg_s[oc * 128:(oc + 1) * 128, half * SLAB:(half + 1) * SLAB])],
                gate=[in_free[b4]] + (em.stream_tok("sg") if s < 4 else []) +
                ([em.tcnt("tensor")] if s < 4 else []))
            i1 = ipool[:, 0:SLAB]
            f1 = xw[:, 0:SLAB] if b == 0 else xw[:, 3 * SLAB:4 * SLAB]
            f2 = xw[:, SLAB:2 * SLAB] if b == 0 else stat[:, 0:SLAB]
            f3 = xw[:, 2 * SLAB:3 * SLAB] if b == 0 else otile[:, 0:SLAB]
            t_a = None
            for c in range(SLAB // 256):
                t_a = em.op("scalar", lambda E, insl=insl, f1=f1, c=c: E.activation(
                    f1[:, c * 256:(c + 1) * 256], insl[:, c * 256:(c + 1) * 256],
                    AF.Identity, scale=colt[:, 1:2]),
                    waits=[ld, tc_cg, fb_free[b]] if c == 0 else [])
            in_free[b4] = t_a
            t_i1 = em.op("vector", lambda E, i1=i1, f1=f1: E.tensor_copy(i1, f1), waits=[t_a])
            t_f1 = em.op("vector", lambda E, i1=i1, f1=f1: E.tensor_copy(f1, i1))
            t_sg = None
            for c in range(max(1, SLAB // 512)):
                t_sg = em.op("scalar", lambda E, f1=f1, f2=f2, c=c: E.activation(
                    f2[:, c * 512:(c + 1) * 512], f1[:, c * 512:(c + 1) * 512],
                    AF.Sigmoid, scale=colt[:, 3:4]),
                    waits=[t_f1] + ([tc_s3, fb2_free[b]] if c == 0 else []))
            t_i2 = em.op("vector", lambda E, f2=f2, i1=i1: E.tensor_scalar(
                i1, f2, 65536.0, None, ALU.mult), waits=[t_sg])
            fb2_free[b] = t_i2
            t_t4 = em.op("vector", lambda E, i1=i1, f1=f1, f3=f3: E.scalar_tensor_tensor(
                f3, i1, 65535.0, f1, ALU.min, ALU.mult), waits=[t_i2])
            fb_free[b] = t_t4
            t_cv = em.op("vector", lambda E, f3=f3, outsl=outsl: E.tensor_copy(outsl, f3),
                         waits=[t_t4, out_free[b]])
            em.op("vector", lambda E, f3=f3: E.tensor_reduce(
                rmax[:, 2:3], f3, AX.X, ALU.max, apply_absolute_value=True))
            t_rm4 = em.op("vector", lambda E: E.tensor_tensor(rmax[:, 5:6], rmax[:, 5:6],
                                                              rmax[:, 2:3], ALU.max))
            st = em.dma_group("t4w", [
                (t4_s[oc * 128:(oc + 1) * 128, half * SLAB:(half + 1) * SLAB], outsl)],
                gate=[t_cv])
            out_free[b] = st
            t4_store[s] = st

        t_m4 = cross_core_max(rmax[:, 5:6], 2, 1, 9, t_rm4)
        t_c = s_mul(cell(16), cell(9), cell(6), waits=[t_m4, t_s3c])
        t_c = s_mul(cell(16), cell(16), 1.0 / (65536.0 * QMAX), waits=[t_c])
        t_s4c = s_maxeps(cell(10), cell(16), waits=[t_c])
        t_r4 = s_recip(cell(20), cell(10), waits=[t_s4c])
        t_c34 = s_mul(cell(11), cell(6), cell(20), waits=[t_r4])
        t_c34 = s_mul(cell(11), cell(11), 1.0 / 65536.0, waits=[t_c34])
        tc_c34 = bcast_col(cell(11), colt[:, 4:5], t_c34)

        if phases <= 6:
            em.waits("sync", em.stream_tok("t4w"))
            em.waits("vector", [tc_c34])
            em.finalize(ctx.enter_context(nc.Block()))
            return nc
        in2_free = [None, None, None, None]
        t5_store = [None] * (HFC * NSLAB)
        t_rm5 = None
        for s in range(HFC * NSLAB):
            oc, half = divmod(s, NSLAB)
            b = s % 2
            b4 = s % 4
            insl = ffnC[:, 8192 + b4 * SLAB:8192 + (b4 + 1) * SLAB]
            insl2 = ffnC[:, b4 * SLAB:(b4 + 1) * SLAB]
            outsl = ffnC[:, 16384 + b * SLAB:16384 + (b + 1) * SLAB]
            ld1 = em.dma_group("t4r", [
                (insl, t4_s[oc * 128:(oc + 1) * 128, half * SLAB:(half + 1) * SLAB])],
                gate=[in_free[b4], t4_store[s]])
            ld2 = em.dma_group("pur", [
                (insl2, pu_s[oc * 128:(oc + 1) * 128, half * SLAB:(half + 1) * SLAB])],
                gate=[in2_free[b4]] + ([em.tcnt("tensor")] if s < 4 else []))
            i1 = ipool[:, 0:SLAB]
            f1 = xw[:, 0:SLAB] if b == 0 else xw[:, 3 * SLAB:4 * SLAB]
            f2 = xw[:, SLAB:2 * SLAB] if b == 0 else stat[:, 0:SLAB]
            f3 = xw[:, 2 * SLAB:3 * SLAB] if b == 0 else otile[:, 0:SLAB]
            t_a = None
            for c in range(SLAB // 256):
                t_a = em.op("scalar", lambda E, insl=insl, f1=f1, c=c: E.activation(
                    f1[:, c * 256:(c + 1) * 256], insl[:, c * 256:(c + 1) * 256],
                    AF.Identity, scale=colt[:, 4:5]),
                    waits=[ld1, tc_c34, fb_free[b]] if c == 0 else [])
            in_free[b4] = t_a
            t_i3 = em.op("vector", lambda E, i1=i1, f1=f1: E.tensor_copy(i1, f1), waits=[t_a])
            t_f = em.op("vector", lambda E, i1=i1, f1=f1: E.tensor_copy(f1, i1))
            t_b = None
            for c in range(SLAB // 256):
                t_b = em.op("scalar", lambda E, insl2=insl2, f2=f2, c=c: E.activation(
                    f2[:, c * 256:(c + 1) * 256], insl2[:, c * 256:(c + 1) * 256],
                    AF.Identity, scale=colt[:, 2:3]),
                    waits=[ld2, tc_cu, fb2_free[b]] if c == 0 else [])
            in2_free[b4] = t_b
            t_i4 = em.op("vector", lambda E, i1=i1, f2=f2: E.tensor_copy(i1, f2),
                         waits=[t_b, t_f])
            fb2_free[b] = t_i4
            t_t5 = em.op("vector", lambda E, i1=i1, f1=f1, f3=f3: E.scalar_tensor_tensor(
                f3, i1, 1.0, f1, ALU.bypass, ALU.mult), waits=[t_i4])
            fb_free[b] = t_t5
            t_cv = em.op("vector", lambda E, f3=f3, outsl=outsl: E.tensor_copy(outsl, f3),
                         waits=[t_t5, out_free[b]])
            em.op("vector", lambda E, f3=f3: E.tensor_reduce(
                rmax[:, 2:3], f3, AX.X, ALU.max, apply_absolute_value=True))
            t_rm5 = em.op("vector", lambda E: E.tensor_tensor(rmax[:, 6:7], rmax[:, 6:7],
                                                              rmax[:, 2:3], ALU.max))
            st = em.dma_group("t5w", [
                (t5_s[oc * 128:(oc + 1) * 128, half * SLAB:(half + 1) * SLAB], outsl)],
                gate=[t_cv])
            out_free[b] = st
            t5_store[s] = st

        t_m5 = cross_core_max(rmax[:, 6:7], 3, 1, 12, t_rm5)
        t_d = s_mul(cell(16), cell(12), cell(10), waits=[t_m5, t_s4c])
        t_d = s_mul(cell(16), cell(16), cell(5), waits=[t_d, t_s2c])
        t_d = s_mul(cell(16), cell(16), 1.0 / QMAX, waits=[t_d])
        t_s5c = s_maxeps(cell(13), cell(16), waits=[t_d])
        t_r5 = s_recip(cell(21), cell(13), waits=[t_s5c])
        t_c45 = s_mul(cell(14), cell(10), cell(5), waits=[t_r5])
        t_c45 = s_mul(cell(14), cell(14), cell(21), waits=[t_c45])
        tc_c45 = bcast_col(cell(14), colt[:, 5:6], t_c45)
        tc_s5 = bcast_col(cell(13), colt[:, 6:7], t_s5c)
        t_gbs5 = em.op("vector", lambda E: E.tensor_tensor(
            cgb[:, 40:48], GBP, colt[:, 6:7].broadcast_to([128, 8]), ALU.mult),
            waits=[tc_s5, t_gab2], strict=True)
        if taps:
            em.dma_group("sg", [(dbg[0:1, 0:32], cells[0:1, 0:32])], gate=[t_gbs5])

        if phases <= 7:
            em.waits("sync", em.stream_tok("t5w"))
            em.waits("vector", [tc_c45])
            em.finalize(ctx.enter_context(nc.Block()))
            return nc
        oq_free = [None, None, None, None]
        d_mark = [None, None, None, None]
        st_free = [None, None]
        last_store = None
        prev_ep = None
        xn_gate = None
        for tb in range(T // NQ):
            ldxn = em.dma_group("x", [
                (xw[:, ic * NQ:(ic + 1) * NQ], outT[ic * 128:(ic + 1) * 128, tb * NQ:(tb + 1) * NQ])
                for ic in range(DC)],
                gate=[em.tcnt("vector"), em.tcnt("scalar"), xn_gate])
            t_dmm = None
            for ic in range(HFC):
                b = ic % 2
                b4 = ic % 4
                tsl = ffnC[:, b4 * NQ:(b4 + 1) * NQ]
                isl = ipool[:, 0:NQ]
                osl = ffnC[:, 8192 + b4 * NQ:8192 + (b4 + 1) * NQ]
                ld = em.dma_group("t5r", [
                    (tsl, t5_s[ic * 128:(ic + 1) * 128, tb * NQ:(tb + 1) * NQ])],
                    gate=[oq_free[b4]] + ([em.stream_tok("t5w"), em.tcnt("vector"),
                                          em.tcnt("scalar")]
                                         if (tb == 0 and ic < 4) else []))
                t_a = None
                for c in range(NQ // 256):
                    t_a = em.op("scalar", lambda E, tsl=tsl, c=c, b=b: E.activation(
                        stat[:, b * NQ + c * 256:b * NQ + (c + 1) * 256],
                        tsl[:, c * 256:(c + 1) * 256], AF.Identity, scale=colt[:, 5:6]),
                        waits=[ld, tc_c45, st_free[b]] if c == 0 else [])
                t_i = em.op("vector", lambda E, isl=isl, b=b: E.tensor_copy(
                    isl, stat[:, b * NQ:(b + 1) * NQ]), waits=[t_a])
                st_free[b] = t_i
                t_oq = em.op("vector", lambda E, isl=isl, osl=osl: E.tensor_copy(osl, isl),
                             waits=[t_i, d_mark[b4]])
                oq_free[b4] = t_oq
                for uc in range(DC):
                    em.op("tensor", lambda E, ic=ic, uc=uc, osl=osl: E.matmul(
                        pbank[uc][:, 0:NQ], wd_bf[:, ic * D + uc * 128:ic * D + (uc + 1) * 128],
                        osl, start=(ic == 0), stop=(ic == HFC - 1), skip_group_check=True),
                        waits=(sum((em.bankw(k, []) for k in range(8)), []) +
                               _flat([t_oq, tok_wd])) if (uc == 0 and ic == 0)
                        else ([t_oq] if uc == 0 else []),
                        inc=False)
                d_mark[b4] = em.mark("tensor")
            t_dmm = d_mark[(HFC - 1) % 4]
            for bk in range(8):
                em.bankclr(bk)
            for uc in range(DC):
                esl = e1t[:, (uc % 2) * NQ:(uc % 2 + 1) * NQ]
                if uc % 2 == 0:
                    t_e = em.op("scalar", lambda E, uc=uc, esl=esl: E.activation(
                        esl, pbank[uc][:, 0:NQ], AF.Identity, scale=cgb[:, 40 + uc:41 + uc]),
                        waits=([t_dmm, t_gbs5] if uc == 0 else []) + [e1_free[uc % 2]])
                else:
                    t_e = em.op("vector", lambda E, uc=uc, esl=esl: E.tensor_scalar(
                        esl, pbank[uc][:, 0:NQ], cgb[:, 40 + uc:41 + uc], None, ALU.mult),
                        waits=([t_dmm, t_gbs5] if uc == 1 else []) + [e1_free[uc % 2]])
                em.bankr(uc, t_e)
                t_f = em.op("vector", lambda E, uc=uc, esl=esl: E.tensor_tensor(
                    otile[:, (uc % 2) * NQ:(uc % 2 + 1) * NQ],
                    xw[:, uc * NQ:(uc + 1) * NQ], esl, ALU.add),
                    waits=[t_e, ldxn, last_store])
                em.bankr(uc, t_f)
                e1_free[uc % 2] = t_f
                last_store = em.dma_group("fo", [
                    (outT[uc * 128:(uc + 1) * 128, tb * NQ:(tb + 1) * NQ],
                     otile[:, (uc % 2) * NQ:(uc % 2 + 1) * NQ])], gate=[t_f])
            xn_gate = last_store

        em.waits("sync", em.stream_tok("fo"))
        em.finalize(ctx.enter_context(nc.Block()))
    return nc


def _relayout(inputs):
    f = np.float32
    w = {}
    w["wqT"] = np.ascontiguousarray(np.asarray(inputs["L0_attn_wq"]).T.astype(f))
    w["wkT"] = np.ascontiguousarray(np.asarray(inputs["L0_attn_wk"]).T.astype(f))
    w["wvT"] = np.ascontiguousarray(np.asarray(inputs["L0_attn_wv"]).T.astype(f))
    w["w1T"] = np.ascontiguousarray(np.asarray(inputs["L0_attn_w1"]).T.astype(f))
    w["w2T"] = np.ascontiguousarray(np.asarray(inputs["L0_attn_w2"]).T.astype(f))
    w["wuT"] = np.ascontiguousarray(np.asarray(inputs["L0_ffn_w_up"]).T.astype(f))
    w["wgT"] = np.ascontiguousarray(np.asarray(inputs["L0_ffn_w_gate"]).T.astype(f))
    w["wdT"] = np.ascontiguousarray(np.asarray(inputs["L0_ffn_w_down"]).T.astype(f))
    w["wmod"] = np.ascontiguousarray(np.asarray(inputs["L0_wmod"]).astype(f))

    def colmajor(v, n):
        return np.ascontiguousarray(np.asarray(v).astype(f).reshape(n, 128).T)

    w["g1"] = colmajor(inputs["L0_attn_g1"], 8)
    w["be1"] = colmajor(inputs["L0_attn_be1"], 8)
    w["g2"] = colmajor(inputs["L0_attn_g2"], 8)
    w["be2"] = colmajor(inputs["L0_attn_be2"], 8)
    w["b2"] = colmajor(inputs["L0_attn_b2"], 8)
    w["b1"] = colmajor(inputs["L0_attn_b1"], 32)
    return w


def kernel(**inputs):
    shared = _relayout(inputs)
    x = np.asarray(inputs["L0_attn_x"]).astype(np.float32)
    c = np.asarray(inputs["L0_c"]).astype(np.float32)
    in_maps = []
    for i in range(NCORES):
        m = dict(shared)
        m["xT"] = np.ascontiguousarray(x[i].T)
        m["cc"] = np.ascontiguousarray(c[i].reshape(8, 128).T)
        in_maps.append(m)
    nc = build(taps=False)
    res = run_bass_kernel_spmd(nc, in_maps, core_ids=list(range(NCORES)))
    out = np.stack([np.asarray(r["outT"]).T for r in res.results], axis=0)
    return out.astype(np.float32)


# version 09
# speedup vs baseline: 1.0047x; 1.0047x over previous
import contextlib

import numpy as np
import ml_dtypes

import concourse.bass as bass
import concourse.mybir as mybir
from concourse.bass_utils import run_bass_kernel_spmd

F32 = mybir.dt.float32
BF16 = mybir.dt.bfloat16
I32 = mybir.dt.int32
AF = mybir.ActivationFunctionType
ALU = mybir.AluOpType
AX = mybir.AxisListType

D, T, DC = 1024, 4096, 8
NH = 8
H1, H1C = 4096, 32
HF, HFC = 3072, 24
TW = 256
NWIN = T // TW
NQ = 512
QMAX = 32767.0
EPS_LN, EPS_ATTN = 1e-5, 1e-6
NCORES = 8
SLAB = 1024
NSLAB = T // SLAB

ENGS = ("sync", "scalar", "vector", "tensor", "gpsimd")


def _flat(toks):
    out = []
    for t in toks:
        if t is None:
            continue
        if isinstance(t, list):
            out.extend(x for x in t if x is not None)
        else:
            out.append(t)
    return out


class Em:
    def __init__(self, nc, ctx, selfsync=False):
        self.selfsync = selfsync
        self.nc = nc
        self.ctx = ctx
        self.streams = {e: [] for e in ENGS}
        self.sems = {}
        self.cnt = {}
        self.bank_free = {}
        self.strm = {}
        for e in ENGS:
            self.new_sem("e_" + e)

    def new_sem(self, name):
        h = self.ctx.enter_context(self.nc.semaphore(name))
        self.sems[name] = h
        self.cnt[name] = 0
        return name

    def raw(self, eng, fn):
        self.streams[eng].append(fn)

    def waits(self, eng, toks):
        best = {}
        for n, v in _flat(toks):
            if n == "e_" + eng and not self.selfsync:
                continue
            best[n] = max(best.get(n, 0), v)
        for n, v in sorted(best.items()):
            h = self.sems[n]
            self.raw(eng, lambda E, h=h, v=v: E.wait_ge(h, v))

    def op(self, eng, fn, waits=(), inc=True, strict=False):
        if (self.selfsync or strict) and eng in ("vector", "scalar", "gpsimd"):
            n = "e_" + eng
            if self.cnt[n]:
                self.raw(eng, lambda E, h=self.sems[n], v=self.cnt[n]: E.wait_ge(h, v))
        self.waits(eng, waits)
        if inc:
            n = "e_" + eng
            self.cnt[n] += 1
            v = self.cnt[n]
            h = self.sems[n]
            self.raw(eng, lambda E, fn=fn, h=h: fn(E).then_inc(h, 1))
            return (n, v)
        self.raw(eng, lambda E, fn=fn: fn(E))
        return None

    def mark(self, eng):
        n = "e_" + eng
        self.cnt[n] += 1
        v = self.cnt[n]
        h = self.sems[n]
        last = self.streams[eng].pop()
        self.raw(eng, lambda E, last=last, h=h: last(E).then_inc(h, 1))
        return (n, v)

    def tcnt(self, eng):
        n = "e_" + eng
        return (n, self.cnt[n])

    def stream(self, name, q, nbuf=2):
        self.strm[name] = {"q": q, "sems": [self.new_sem(f"{name}_{i}") for i in range(nbuf)],
                           "idx": 0, "nbuf": nbuf}

    def stream_tok(self, name):
        st = self.strm[name]
        return [(s, self.cnt[s]) for s in st["sems"] if self.cnt[s]]

    def dma_group(self, sname, pairs, gate=()):
        st = self.strm[sname]
        i = st["idx"] % st["nbuf"]
        sem = st["sems"][i]
        prior = self.cnt[sem]
        g = _flat(gate) + ([(sem, prior)] if prior else [])
        self.waits(st["q"], g)
        h = self.sems[sem]
        for out, in_ in pairs:
            self.cnt[sem] += 16
            self.raw(st["q"], lambda E, out=out, in_=in_, h=h:
                     E.dma_start(out=out, in_=in_).then_inc(h, 16))
        st["idx"] += 1
        return (sem, self.cnt[sem])

    def bankw(self, key, waits=()):
        w = _flat(waits) + _flat([self.bank_free.get(key)])
        return w

    def bankr(self, key, tok):
        prev = self.bank_free.get(key)
        toks = _flat([prev, tok])
        self.bank_free[key] = toks

    def bankclr(self, key):
        self.bank_free[key] = None

    def finalize(self, block):
        for eng in ENGS:
            sec = getattr(block, eng)

            def body(E, eng=eng):
                for fn in self.streams[eng]:
                    fn(E)

            sec(body)


def build(taps=False, selfsync=False, phases=99):
    nc = bass.Bass("TRN2", target_bir_lowering=False,
                   detect_race_conditions=bool(selfsync))
    ctx = contextlib.ExitStack()

    def din(name, shape):
        return nc.dram_tensor(name, list(shape), F32, kind="ExternalInput").ap()

    xT = din("xT", (D, T))
    wqT = din("wqT", (D, D)); wkT = din("wkT", (D, D)); wvT = din("wvT", (D, D))
    w1T = din("w1T", (D, H1)); w2T = din("w2T", (H1, D))
    wuT = din("wuT", (D, HF)); wgT = din("wgT", (D, HF)); wdT = din("wdT", (HF, D))
    wmod = din("wmod", (D, 6 * D))
    cc_in = din("cc", (128, 8))
    vg1 = din("g1", (128, 8)); vbe1 = din("be1", (128, 8))
    vg2 = din("g2", (128, 8)); vbe2 = din("be2", (128, 8))
    vb1 = din("b1", (128, 32)); vb2 = din("b2", (128, 8))

    outT = nc.dram_tensor("outT", [D, T], F32, kind="ExternalOutput").ap()

    skind = "ExternalOutput" if taps else "Internal"
    h_s = nc.dram_tensor("h_s", [D, T], BF16, kind=skind).ap()
    qp_s = nc.dram_tensor("qp_s", [D, T], BF16, kind=skind).ap()
    xm_s = nc.dram_tensor("xm_s", [D, T], BF16, kind=skind).ap()
    pu_s = nc.dram_tensor("pu_s", [HF, T], BF16, kind=skind).ap()
    pg_s = nc.dram_tensor("pg_s", [HF, T], BF16, kind=skind).ap()
    t4_s = nc.dram_tensor("t4_s", [HF, T], BF16, kind=skind).ap()
    t5_s = nc.dram_tensor("t5_s", [HF, T], BF16, kind=skind).ap()
    dbg = nc.dram_tensor("dbg", [1, 64], F32, kind=skind).ap()
    xn_s = nc.dram_tensor("xn_s", [D, T], F32, kind=skind).ap()
    z_s = nc.dram_tensor("z_s", [NH, T], F32, kind=skind).ap()
    kc_s = nc.dram_tensor("kc_s", [128, 8], F32, kind=skind).ap()
    kr_s = nc.dram_tensor("kr_s", [128, 1024], BF16, kind=skind).ap()
    ac_s = nc.dram_tensor("ac_s", [D, T], F32, kind=skind).ap()
    bc_d = nc.dram_tensor("bc_d", [64, 128], F32, kind="Internal").ap()
    ar_in = [nc.dram_tensor(f"ar_in{i}", [2], F32, kind="Internal") for i in range(4)]
    ar_out = [nc.dram_tensor(f"ar_out{i}", [2], F32, kind="Internal") for i in range(4)]

    ones128_d = nc.inline_tensor(np.ones((128, 128), dtype=ml_dtypes.bfloat16), name="ones128_d")
    onesrow_d = nc.inline_tensor(np.ones((1, 128), dtype=np.float32), name="onesrow_d")
    ident1_d = nc.inline_tensor(np.ones((1, 1), dtype=np.float32), name="ident1_d")
    epsln_d = nc.inline_tensor(np.full((128, 1), EPS_LN, dtype=np.float32), name="epsln_d")
    epsattn_d = nc.inline_tensor(np.full((128, 1), EPS_ATTN, dtype=np.float32), name="epsattn_d")

    def sb(name, shape, dtype=F32):
        return ctx.enter_context(nc.sbuf_tensor(name, list(shape), dtype))

    def vsb(name, shape, dtype, backing):
        v = nc.virtual_sbuf_tensor(name, list(shape), dtype)
        nc.back_virtual_tensor_with(v, backing)
        return v

    with ctx:
        poolA = sb("poolA", (128, 32768), BF16)
        w1_bf = vsb("w1_bf", (128, 32768), BF16, poolA)
        h2_bf = vsb("h2_bf", (128, 32768), BF16, poolA)
        wd_bf = vsb("wd_bf", (128, 32768), BF16, poolA)
        poolC = sb("poolC", (128, 24576), BF16)
        bootC = vsb("bootC", (128, 24576), BF16, poolC)
        wqkv_bf = vsb("wqkv_bf", (128, 24576), BF16, poolC)
        w2a_bf = vsb("w2a_bf", (128, 24576), BF16, poolC)
        ffnC = vsb("ffnC", (128, 24576), BF16, poolC)
        w2b_bf = sb("w2b_bf", (128, 8192), BF16)

        xw = sb("xw", (128, 2 * DC * TW))
        sq_bf = sb("sq_bf", (128, DC * TW), BF16)
        xbf = sb("xbf", (128, DC * TW), BF16)
        tnorm = sb("tnorm", (128, DC * TW), BF16)
        stat = sb("stat", (128, 4 * TW))
        hw = sb("hw", (128, 2 * DC * TW), BF16)
        stg = vsb("stg", (128, 2 * DC * TW), BF16, hw)
        xbw = sb("xbw", (128, DC * TW), BF16)
        qpw = sb("qpw", (128, 2 * DC * TW), BF16)
        ipool = sb("ipool", (128, SLAB), I32)
        ktok = sb("ktok", (128, D), BF16)
        vtok = sb("vtok", (128, D), BF16)
        kv_bf = sb("kv_bf", (128, NH * 128), BF16)
        kcols = sb("kcols", (128, 8))
        ksrep_bf = sb("ksrep_bf", (128, NH * 128), BF16)
        zw = sb("zw", (128, TW))
        glt = sb("glt", (128, 2 * TW), BF16)
        e1t = sb("e1t", (128, 2 * NQ))
        escr = vsb("escr", (128, 2 * NQ), F32, e1t)
        ksr_bf = vsb("ksr_al", (1, D), F32, e1t)
        otile = sb("otile", (128, 2 * NQ))
        consts = sb("consts", (128, 64))
        cgb = sb("cgb", (128, 48))
        cvec = sb("cvec", (128, 80))
        ccs = sb("ccs", (128, 8))
        siluc_bf = sb("siluc_bf", (128, 8), BF16)
        ones128_bf = sb("ones128_bf", (128, 128), BF16)
        onesrow = sb("onesrow", (1, 128))
        ident1_bf = sb("ident1_bf", (1, 1))
        cells = sb("cells", (1, 32))
        epsln_col = sb("epsln_col", (128, 1))
        epsattn_col = sb("epsattn_col", (128, 1))
        rowt = sb("rowt", (1, 384))
        colt = sb("colt", (128, 16))
        rmax = sb("rmax", (128, 8))

        rem = nc.sbuf_bytes_remaining
        assert rem >= 0, f"SBUF over budget by {-rem} bytes"

        pbank = [ctx.enter_context(nc.psum_tensor(f"pb{i}", [128, NQ], F32)) for i in range(8)]
        PB = list(range(8))

        em = Em(nc, ctx, selfsync=selfsync)
        em.new_sem("ccdone")
        for name, q, nb in (("c0", "sync", 1), ("wm", "gpsimd", 2), ("wq3", "gpsimd", 1),
                            ("w1g", "gpsimd", 1), ("w2a", "gpsimd", 1), ("wd", "gpsimd", 1),
                            ("x", "sync", 2), ("hs", "gpsimd", 2), ("hr", "sync", 2),
                            ("qs", "gpsimd", 2), ("qr", "sync", 2), ("xs", "gpsimd", 2),
                            ("xr", "sync", 2), ("os", "gpsimd", 2), ("ff", "gpsimd", 2),
                            ("sg", "gpsimd", 4), ("pg", "sync", 4), ("t4w", "gpsimd", 4),
                            ("t4r", "sync", 4), ("pur", "sync", 4), ("t5w", "gpsimd", 4),
                            ("t5r", "sync", 4), ("fo", "gpsimd", 4), ("tz", "sync", 1),
                            ("bc", "gpsimd", 1), ("cc", "gpsimd", 1)):
            em.stream(name, q, nb)

        tok_consts = em.dma_group("c0", [
            (ccs[:, :], cc_in),
            (cvec[:, 0:8], vg1), (cvec[:, 8:16], vbe1), (cvec[:, 16:24], vg2),
            (cvec[:, 24:32], vbe2), (cvec[:, 32:40], vb2), (cvec[:, 40:72], vb1),
            (ones128_bf[:, :], ones128_d.ap()), (onesrow[:, :], onesrow_d.ap()),
            (ident1_bf[:, :], ident1_d.ap()), (epsln_col[:, :], epsln_d.ap()),
            (epsattn_col[:, :], epsattn_d.ap())])
        tk_cc = tk_vec = tk_misc = tk_eps = tok_consts
        COL_G1, COL_BE1, COL_G2, COL_BE2, COL_B2, COL_B1 = 0, 8, 16, 24, 32, 40

        def g1c(ic): return cvec[:, COL_G1 + ic:COL_G1 + ic + 1]
        def be1c(ic): return cvec[:, COL_BE1 + ic:COL_BE1 + ic + 1]
        def g2c(ic): return cvec[:, COL_G2 + ic:COL_G2 + ic + 1]
        def be2c(ic): return cvec[:, COL_BE2 + ic:COL_BE2 + ic + 1]
        def b1c(oc): return cvec[:, COL_B1 + oc:COL_B1 + oc + 1]

        t_silu = em.op("scalar", lambda E: E.activation(siluc_bf[:, :], ccs[:, :], AF.Silu),
                       waits=[tk_cc])
        mmtok = None
        prev_boot = None
        for ow in range(12):
            buf = ow % 2
            wslc = bootC[:, buf * 4096:(buf + 1) * 4096]
            gate = em.tcnt("tensor") if ow >= 2 else None
            ld = em.dma_group("wm", [
                (wslc[:, ic * 512:(ic + 1) * 512],
                 wmod[ic * 128:(ic + 1) * 128, ow * 512:(ow + 1) * 512])
                for ic in range(DC)], gate=[gate])
            for oc_rel in range(4):
                oc = ow * 4 + oc_rel
                for ic in range(DC):
                    em.op("tensor", lambda E, oc=oc, oc_rel=oc_rel, ic=ic, wslc=wslc: E.matmul(
                        pbank[0][:, oc:oc + 1],
                        wslc[:, ic * 512 + oc_rel * 128:ic * 512 + oc_rel * 128 + 128],
                        siluc_bf[:, ic:ic + 1],
                        start=(ic == 0), stop=(ic == DC - 1)),
                        waits=em.bankw(0, [ld, t_silu]) if (ic == 0 and oc_rel == 0) else [],
                        inc=False)
            mmtok = em.mark("tensor")
        t_modv = em.op("vector", lambda E: E.tensor_copy(consts[:, 0:48], pbank[0][:, 0:48]),
                       waits=[mmtok])
        em.bankr(0, t_modv)
        em.op("vector", lambda E: E.tensor_scalar(cgb[:, 0:8], consts[:, 8:16], 1.0, None, ALU.add),
              waits=[t_modv])
        em.op("vector", lambda E: E.tensor_copy(cgb[:, 8:16], consts[:, 0:8]))
        em.op("vector", lambda E: E.tensor_scalar(cgb[:, 16:24], consts[:, 32:40], 1.0, None, ALU.add))
        em.op("vector", lambda E: E.tensor_copy(cgb[:, 24:32], consts[:, 24:32]))

        t_quart = em.op("vector", lambda E: E.memset(colt[:, 7:8], 0.25))
        t_quart = em.op("vector", lambda E: E.memset(colt[:, 8:9], -0.25), waits=[t_quart])

        def clamp_gate(src_sl, dst_sl, tmp_sl, tmp2_sl, sgn_sl, prev):
            t0 = em.op("scalar", lambda E: E.activation(sgn_sl, src_sl, AF.Sign),
                       waits=[t_modv, prev])
            t1 = em.op("scalar", lambda E: E.activation(tmp_sl, src_sl, AF.Abs))
            t2 = em.op("scalar", lambda E: E.activation(tmp_sl, tmp_sl, AF.Relu,
                                                        bias=colt[:, 8:9]), waits=[t_quart])
            t3 = em.op("scalar", lambda E: E.activation(tmp2_sl, tmp_sl, AF.Identity,
                                                        bias=colt[:, 7:8]))
            return em.op("vector", lambda E: E.tensor_tensor(dst_sl, sgn_sl, tmp2_sl, ALU.mult),
                         waits=[t0, t3])

        t_ga = clamp_gate(consts[:, 16:24], cgb[:, 32:40], consts[:, 56:64], rmax[:, 0:8],
                          stat[:, 0:8], None)
        t_gb = clamp_gate(consts[:, 40:48], consts[:, 48:56], consts[:, 56:64], rmax[:, 0:8],
                          stat[:, 0:8], t_ga)
        GBP = consts[:, 48:56]
        t_gab2 = em.op("vector", lambda E: E.tensor_tensor(
            cgb[:, 40:48], cgb[:, 32:40], cvec[:, COL_B2:COL_B2 + 8], ALU.mult),
            waits=[tk_vec, t_gb])
        tok_mod_done = t_gab2

        def wpairs(dst, src_ap, icount, ocount):
            return [(dst[:, ic * ocount:(ic + 1) * ocount],
                     src_ap[ic * 128:(ic + 1) * 128, 0:ocount]) for ic in range(icount)]

        tok_wq3 = em.dma_group("wq3",
                               wpairs(wqkv_bf[:, 0:8192], wqT, DC, D) +
                               wpairs(wqkv_bf[:, 8192:16384], wkT, DC, D) +
                               wpairs(wqkv_bf[:, 16384:24576], wvT, DC, D), gate=[mmtok])
        tok_wq = tok_wk = tok_wv = tok_wq3
        tok_w1g = em.dma_group("w1g",
                               wpairs(w1_bf, w1T, DC, H1) +
                               wpairs(w2b_bf, w2T[24 * 128:32 * 128, 0:D], 8, D))
        tok_w1 = tok_w2b = tok_w1g

        def wq_blk(m, ic, oc):
            base = {0: 0, 1: 8192, 2: 16384}[m]
            return wqkv_bf[:, base + ic * D + oc * 128: base + ic * D + (oc + 1) * 128]

        state = {"sq_free": None, "xbf_free": None, "stats_free": None, "tnorm_free": None}

        def ln_stats(rhs_tile, sqsrc_ap, pre_toks):
            t_sq = em.op("scalar", lambda E: E.activation(sq_bf[:, :], sqsrc_ap, AF.Square),
                         waits=_flat(pre_toks) + _flat([state["sq_free"]]))
            for ic in range(DC):
                em.op("tensor", lambda E, ic=ic: E.matmul(
                    pbank[0][:, 0:TW], ones128_bf[:, :], rhs_tile[:, ic * TW:(ic + 1) * TW],
                    start=(ic == 0), stop=(ic == DC - 1)),
                    waits=em.bankw(0, pre_toks) if ic == 0 else [], inc=False)
            for ic in range(DC):
                em.op("tensor", lambda E, ic=ic: E.matmul(
                    pbank[1][:, 0:TW], ones128_bf[:, :], sq_bf[:, ic * TW:(ic + 1) * TW],
                    start=(ic == 0), stop=(ic == DC - 1)),
                    waits=em.bankw(1, [t_sq]) if ic == 0 else [], inc=False)
            t_mm = em.mark("tensor")
            state["sq_free"] = t_mm
            mu, r, mur, tmp = (stat[:, i * TW:(i + 1) * TW] for i in range(4))
            em.op("vector", lambda E: E.tensor_scalar(mu, pbank[0][:, 0:TW], 1.0 / D, None, ALU.mult),
                  waits=[t_mm])
            em.op("vector", lambda E: E.scalar_tensor_tensor(tmp, mu, 1.0, mu, ALU.bypass, ALU.mult))
            t_var = em.op("vector", lambda E: E.scalar_tensor_tensor(
                r, pbank[1][:, 0:TW], 1.0 / D, tmp, ALU.mult, ALU.subtract))
            em.bankr(0, t_var)
            em.bankr(1, t_var)
            t_sd = em.op("scalar", lambda E: E.activation(tmp, r, AF.Sqrt, bias=epsln_col[:, :]),
                         waits=[t_var])
            em.op("vector", lambda E: E.reciprocal(r, tmp), waits=[t_sd])
            t_st = em.op("vector", lambda E: E.tensor_tensor(mur, mu, r, ALU.mult))
            return t_st, r, mur

        def ln_apply(src_ap, r_sl, mur_sl, stat_tok, pre_toks=()):
            src3 = src_ap.rearrange("p (c t) -> p c t", c=DC)
            t3 = tnorm[:, :].rearrange("p (c t) -> p c t", c=DC)
            r3 = r_sl.unsqueeze(1).broadcast_to([128, DC, TW])
            m3 = mur_sl.unsqueeze(1).broadcast_to([128, DC, TW])
            em.op("vector", lambda E: E.scalar_tensor_tensor(t3, src3, 1.0, r3, ALU.bypass, ALU.mult),
                  waits=_flat([stat_tok]) + _flat(pre_toks) + _flat([state["tnorm_free"]]))
            return em.op("vector", lambda E: E.tensor_tensor(t3, t3, m3, ALU.subtract))

        def affine_out(dst_fn, scale_fn, bias_fn, t_norm, extra=()):
            toks = []
            for ic in range(DC):
                toks.append(em.op("scalar", lambda E, ic=ic: E.activation(
                    dst_fn(ic), tnorm[:, ic * TW:(ic + 1) * TW],
                    AF.Identity, scale=scale_fn(ic), bias=bias_fn(ic)),
                    waits=(_flat([t_norm]) + _flat(extra)) if ic == 0 else []))
            state["tnorm_free"] = toks[-1]
            return toks[-1]

        xw_free = [None, None]
        hw_write_gate = [None, None]
        qp_write_gate = [None, None]
        h_spill = None
        qp_spill = None
        prev_kv_mark = None

        for w in range(NWIN):
            buf = w % 2
            t0 = w * TW
            xsl = xw[:, buf * DC * TW:(buf + 1) * DC * TW]
            hsl = hw[:, buf * DC * TW:(buf + 1) * DC * TW]
            tok_x = em.dma_group("x", [
                (xsl[:, ic * TW:(ic + 1) * TW], xT[ic * 128:(ic + 1) * 128, t0:t0 + TW])
                for ic in range(DC)], gate=[xw_free[buf]])
            t_xbf = em.op("vector", lambda E, xsl=xsl: E.tensor_copy(xbf[:, :], xsl),
                          waits=[tok_x, tok_mod_done, state["xbf_free"]])
            st_tok, r_sl, mur_sl = ln_stats(xbf[:, :], xsl, [t_xbf, tok_x])
            state["xbf_free"] = state["sq_free"]
            t_n = ln_apply(xsl, r_sl, mur_sl, st_tok)
            t_h = affine_out(lambda ic, hsl=hsl: hsl[:, ic * TW:(ic + 1) * TW],
                             lambda ic: cgb[:, 0 + ic:1 + ic], lambda ic: cgb[:, 8 + ic:9 + ic],
                             t_n, extra=[hw_write_gate[buf]])
            xw_free[buf] = [t_n, t_h]
            h_spill = em.dma_group("hs", [
                (h_s[ic * 128:(ic + 1) * 128, t0:t0 + TW], hsl[:, ic * TW:(ic + 1) * TW])
                for ic in range(DC)], gate=[t_h])
            st2, r2, mur2 = ln_stats(hsl, hsl[:, :], [t_h])
            t_n2 = ln_apply(hsl, r2, mur2, st2)
            t_x2b = affine_out(lambda ic: xbw[:, ic * TW:(ic + 1) * TW], g1c, be1c, t_n2,
                               extra=[prev_kv_mark])

            qsl = qpw[:, buf * DC * TW:(buf + 1) * DC * TW]
            prev_q = qp_write_gate[buf]
            for oc in range(DC):
                bk = 2 + oc % 2
                for ic in range(DC):
                    em.op("tensor", lambda E, oc=oc, ic=ic, bk=bk: E.matmul(
                        pbank[bk][:, 0:TW], wq_blk(0, ic, oc), xbw[:, ic * TW:(ic + 1) * TW],
                        start=(ic == 0), stop=(ic == DC - 1)),
                        waits=em.bankw(bk, [t_x2b, tok_wq]) if ic == 0 else [], inc=False)
                t_q = em.mark("tensor")
                esl = escr[:, (oc % 4) * TW:(oc % 4 + 1) * TW]
                t_e = em.op("scalar", lambda E, bk=bk, esl=esl: E.activation(esl, pbank[bk][:, 0:TW],
                                                                             AF.Exp), waits=[t_q])
                em.op("vector", lambda E, esl=esl: E.tensor_scalar(esl, esl, 1.0, None, ALU.min),
                      waits=[t_e])
                prev_q = em.op("vector", lambda E, bk=bk, esl=esl, oc=oc, qsl=qsl:
                               E.scalar_tensor_tensor(qsl[:, oc * TW:(oc + 1) * TW], pbank[bk][:, 0:TW],
                                                      0.0, esl, ALU.max, ALU.add),
                               waits=[prev_q] if oc == 0 else [])
                em.bankr(bk, prev_q)
            qp_spill = em.dma_group("qs", [
                (qp_s[ic * 128:(ic + 1) * 128, t0:t0 + TW], qsl[:, ic * TW:(ic + 1) * TW])
                for ic in range(DC)], gate=[prev_q])
            qp_write_gate[buf] = qp_spill

            for sblk in range(TW // 128):
                tb0 = sblk * 128
                for nn in range(2):
                    bk = 2 + nn
                    for ic in range(DC):
                        em.op("tensor", lambda E, nn=nn, ic=ic, bk=bk, tb0=tb0: E.matmul(
                            pbank[bk][:, 0:NQ],
                            xbw[:, ic * TW + tb0:ic * TW + tb0 + 128],
                            wqkv_bf[:, 8192 + ic * D + nn * NQ:8192 + ic * D + (nn + 1) * NQ],
                            start=(ic == 0), stop=(ic == DC - 1)),
                            waits=em.bankw(bk, [t_x2b, tok_wk]) if ic == 0 else [], inc=False)
                t_k = em.mark("tensor")
                tk_done = []
                for nn in range(2):
                    bk = 2 + nn
                    esl = escr[:, nn * NQ:(nn + 1) * NQ]
                    t_e = em.op("scalar", lambda E, bk=bk, esl=esl: E.activation(
                        esl, pbank[bk][:, 0:NQ], AF.Exp), waits=[t_k])
                    em.op("vector", lambda E, esl=esl: E.tensor_scalar(esl, esl, 1.0, None, ALU.min),
                          waits=[t_e])
                    tk = em.op("vector", lambda E, bk=bk, esl=esl, nn=nn:
                               E.scalar_tensor_tensor(ktok[:, nn * NQ:(nn + 1) * NQ],
                                                      pbank[bk][:, 0:NQ], 0.0, esl, ALU.max, ALU.add))
                    em.bankr(bk, tk)
                    tk_done.append(tk)
                for hh in range(NH):
                    em.op("tensor", lambda E, hh=hh, w=w, sblk=sblk: E.matmul(
                        pbank[6][:, hh:hh + 1], ktok[:, hh * 128:(hh + 1) * 128],
                        ones128_bf[:, 0:1],
                        start=(w == 0 and sblk == 0 and hh == 0),
                        stop=(w == NWIN - 1 and sblk == TW // 128 - 1),
                        skip_group_check=True),
                        waits=[tk_done[hh // 4]] if hh in (0, 4) else [], inc=False)
                for nn in range(2):
                    bk = 2 + nn
                    for ic in range(DC):
                        em.op("tensor", lambda E, nn=nn, ic=ic, bk=bk, tb0=tb0: E.matmul(
                            pbank[bk][:, 0:NQ],
                            xbw[:, ic * TW + tb0:ic * TW + tb0 + 128],
                            wqkv_bf[:, 16384 + ic * D + nn * NQ:16384 + ic * D + (nn + 1) * NQ],
                            start=(ic == 0), stop=(ic == DC - 1)),
                            waits=em.bankw(bk, [tok_wv]) if ic == 0 else [], inc=False)
                t_v = em.mark("tensor")
                tv_done = []
                for nn in range(2):
                    bk = 2 + nn
                    tv = em.op("scalar", lambda E, bk=bk, nn=nn:
                               E.copy(vtok[:, nn * NQ:(nn + 1) * NQ], pbank[bk][:, 0:NQ]),
                               waits=[t_v])
                    em.bankr(bk, tv)
                    tv_done.append(tv)
                for hh in range(NH):
                    em.op("tensor", lambda E, hh=hh, w=w, sblk=sblk: E.matmul(
                        pbank[4 + hh // 4][:, (hh % 4) * 128:(hh % 4 + 1) * 128],
                        ktok[:, hh * 128:(hh + 1) * 128], vtok[:, hh * 128:(hh + 1) * 128],
                        start=(w == 0 and sblk == 0 and hh % 4 == 0),
                        stop=(w == NWIN - 1 and sblk == TW // 128 - 1),
                        skip_group_check=True),
                        waits=[tv_done[0], tv_done[1]] if hh == 0 else [], inc=False)
                prev_kv_mark = em.mark("tensor")
            hw_write_gate[buf] = [h_spill, t_n2, prev_kv_mark]

        kv_last = prev_kv_mark

        tok_w2a = em.dma_group("w2a", wpairs(w2a_bf, w2T, 24, D), gate=[kv_last])

        def w2_blk(ic, oc):
            if ic < 24:
                return w2a_bf[:, ic * D + oc * 128:ic * D + (oc + 1) * 128]
            return w2b_bf[:, (ic - 24) * D + oc * 128:(ic - 24) * D + (oc + 1) * 128]

        t_kv0 = em.op("vector", lambda E: E.tensor_copy(kv_bf[:, 0:NQ], pbank[4][:, 0:NQ]),
                      waits=[kv_last])
        t_kv = em.op("vector", lambda E: E.tensor_copy(kv_bf[:, NQ:2 * NQ], pbank[5][:, 0:NQ]))
        em.bankr(4, t_kv0); em.bankr(5, t_kv)
        t_kc = em.op("vector", lambda E: E.tensor_copy(kcols[:, 0:8], pbank[6][:, 0:8]),
                     waits=[kv_last])
        em.bankr(6, t_kc)
        if taps:
            em.dma_group("sg", [(kc_s, cgb[:, 32:40])], gate=[t_kc, tok_mod_done])
        t_krep = None
        for d in range(DC):
            t_krep = em.op("scalar", lambda E, d=d: E.activation(
                ksrep_bf[:, d * 128:(d + 1) * 128], ones128_bf[:, 0:128], AF.Identity,
                scale=kcols[:, d:d + 1]), waits=[t_kc] if d == 0 else [])
        if taps:
            em.dma_group("tz", [(kr_s, ksrep_bf[:, :])], gate=[t_krep])

        qp_rd_gate = [None, None]
        hw_rd_gate = [None, None]
        xm_spill = None
        tapz = None
        for w in range(NWIN):
            buf = w % 2
            t0 = w * TW
            qsl = qpw[:, buf * DC * TW:(buf + 1) * DC * TW]
            hsl = hw[:, buf * DC * TW:(buf + 1) * DC * TW]
            ldq = em.dma_group("qr", [
                (qsl[:, ic * TW:(ic + 1) * TW], qp_s[ic * 128:(ic + 1) * 128, t0:t0 + TW])
                for ic in range(DC)],
                gate=[qp_rd_gate[buf], em.stream_tok("qs")] + ([em.tcnt("tensor")] if w < 2 else []))
            ldh = em.dma_group("hr", [
                (hsl[:, ic * TW:(ic + 1) * TW], h_s[ic * 128:(ic + 1) * 128, t0:t0 + TW])
                for ic in range(DC)],
                gate=[hw_rd_gate[buf], em.stream_tok("hs")] + ([em.tcnt("tensor")] if w < 2 else []))
            den_mark = [None] * NH
            msg_mark = [None] * NH

            def emit_head_mms(hh, qsl=qsl, ldq=ldq):
                dbk = hh % 4
                em.op("tensor", lambda E, hh=hh, qsl=qsl, dbk=dbk: E.matmul(
                    pbank[dbk][:, 0:TW], ksrep_bf[:, hh * 128:(hh + 1) * 128],
                    qsl[:, hh * TW:(hh + 1) * TW], start=True, stop=True),
                    waits=em.bankw(dbk, [ldq, t_krep] if hh == 0 else []), inc=False)
                den_mark[hh] = em.mark("tensor")
                mbk = 4 + hh % 4
                em.op("tensor", lambda E, hh=hh, qsl=qsl, mbk=mbk: E.matmul(
                    pbank[mbk][:, 0:TW], kv_bf[:, hh * 128:(hh + 1) * 128],
                    qsl[:, hh * TW:(hh + 1) * TW], start=True, stop=True),
                    waits=em.bankw(mbk, []), inc=False)
                msg_mark[hh] = em.mark("tensor")

            for hh in range(4):
                emit_head_mms(hh)
            dn_free = [None, None]
            z_free = [None, None]
            ms_free = [None, None]
            t_xm = None
            for hh in range(NH):
                p = hh % 2
                dbk = hh % 4
                mbk = 4 + hh % 4
                dsl = stat[:, p * TW:(p + 1) * TW]
                zsl = zw[:, :] if p == 0 else otile[:, 0:TW]
                msl = stat[:, (2 + p) * TW:(3 + p) * TW]
                t_dn = em.op("scalar", lambda E, dsl=dsl, dbk=dbk: E.activation(
                    dsl, pbank[dbk][:, 0:TW], AF.Identity, bias=epsattn_col[:, :]),
                    waits=[den_mark[hh], dn_free[p]])
                em.bankr(dbk, t_dn)
                t_z = em.op("vector", lambda E, zsl=zsl, dsl=dsl: E.reciprocal(zsl, dsl),
                            waits=[t_dn, z_free[p], tapz])
                dn_free[p] = t_z
                t_ms = em.op("vector", lambda E, msl=msl, zsl=zsl, mbk=mbk:
                             E.scalar_tensor_tensor(msl, pbank[mbk][:, 0:TW],
                                                    1.0, zsl, ALU.bypass, ALU.mult),
                             waits=[msg_mark[hh], ms_free[p]] + ([ldh] if hh == 0 else []))
                em.bankr(mbk, t_ms)
                z_free[p] = t_ms
                t_xm = em.op("vector", lambda E, hh=hh, hsl=hsl, msl=msl: E.tensor_tensor(
                    hsl[:, hh * TW:(hh + 1) * TW], hsl[:, hh * TW:(hh + 1) * TW],
                    msl, ALU.add))
                ms_free[p] = t_xm
                if hh + 4 < NH:
                    emit_head_mms(hh + 4)
                if taps:
                    tapz = em.dma_group("tz", [(z_s[hh:hh + 1, t0:t0 + TW], zsl[0:1, :])],
                                        gate=[t_xm])
            qp_rd_gate[buf] = t_xm
            xm_spill = em.dma_group("xs", [
                (xm_s[ic * 128:(ic + 1) * 128, t0:t0 + TW], hsl[:, ic * TW:(ic + 1) * TW])
                for ic in range(DC)], gate=[t_xm])
            hw_rd_gate[buf] = xm_spill

        if phases <= 2:
            em.waits("sync", em.stream_tok("xs"))
            em.finalize(ctx.enter_context(nc.Block()))
            return nc
        out_spill = None
        xm_rd2 = [None, None]
        otile_gate = [None, None]
        e1_free = [None, None]
        for w in range(NWIN):
            buf = w % 2
            t0 = w * TW
            hsl = hw[:, buf * DC * TW:(buf + 1) * DC * TW]
            xsl = xw[:, buf * DC * TW:(buf + 1) * DC * TW]
            ldxm = em.dma_group("xr", [
                (hsl[:, ic * TW:(ic + 1) * TW], xm_s[ic * 128:(ic + 1) * 128, t0:t0 + TW])
                for ic in range(DC)], gate=[xm_rd2[buf], em.stream_tok("xs")])
            ldx2 = em.dma_group("x", [
                (xsl[:, ic * TW:(ic + 1) * TW], xT[ic * 128:(ic + 1) * 128, t0:t0 + TW])
                for ic in range(DC)], gate=[xw_free[buf]])
            st3, r3, mur3 = ln_stats(hsl, hsl[:, :], [ldxm])
            t_n3 = ln_apply(hsl, r3, mur3, st3)
            t_x2c = affine_out(lambda ic: xbw[:, ic * TW:(ic + 1) * TW], g2c, be2c, t_n3,
                               extra=[em.tcnt("tensor")])
            t_gl_done = [None, None]
            w1tok = [None] * H1C

            def emit_w1(oc):
                bk = 2 + oc % 2
                for ic in range(DC):
                    em.op("tensor", lambda E, oc=oc, ic=ic, bk=bk: E.matmul(
                        pbank[bk][:, 0:TW], w1_bf[:, ic * H1 + oc * 128:ic * H1 + (oc + 1) * 128],
                        xbw[:, ic * TW:(ic + 1) * TW], start=(ic == 0), stop=(ic == DC - 1)),
                        waits=em.bankw(bk, [t_x2c, tok_w1]) if ic == 0 else [], inc=False)
                w1tok[oc] = em.mark("tensor")

            emit_w1(0)
            emit_w1(1)
            for oc in range(H1C):
                bk = 2 + oc % 2
                gsl = qpw[:, (oc % 4) * TW:(oc % 4 + 1) * TW]
                t_gl = em.op("scalar", lambda E, bk=bk, gsl=gsl, oc=oc: E.activation(
                    gsl, pbank[bk][:, 0:TW], AF.Gelu, bias=b1c(oc)), waits=[w1tok[oc]])
                em.bankr(bk, t_gl)
                if oc + 2 < H1C:
                    emit_w1(oc + 2)
                if taps and w == 0 and oc < 4:
                    t_gl = em.dma_group("tz", [(kr_s[0:128, oc * TW:(oc + 1) * TW], gsl)],
                                        gate=[t_gl]) and t_gl
                    em.waits("scalar", em.stream_tok("tz"))
                for uc in range(DC):
                    abk = 4 + uc // 2
                    em.op("tensor", lambda E, uc=uc, oc=oc, gsl=gsl, abk=abk: E.matmul(
                        pbank[abk][:, (uc % 2) * TW:(uc % 2 + 1) * TW],
                        w2_blk(oc, uc), gsl, start=(oc == 0 and uc % 2 == 0),
                        stop=(oc == H1C - 1), skip_group_check=True),
                        waits=(em.bankw(4, []) + em.bankw(5, []) + em.bankw(6, []) +
                               em.bankw(7, []) + _flat([t_gl, tok_w2a, tok_w2b]))
                        if (uc == 0 and oc == 0) else ([t_gl] if uc == 0 else []),
                        inc=False)
                t_gl_done[oc % 2] = em.mark("tensor")
            t_acc = t_gl_done[(H1C - 1) % 2]
            for bk in (4, 5, 6, 7):
                em.bankclr(bk)
            t_out = None
            for uc in range(DC):
                abk = 4 + uc // 2
                esl = e1t[:, (uc % 2) * NQ:(uc % 2) * NQ + TW]
                t_e1 = em.op("scalar", lambda E, uc=uc, esl=esl, abk=abk: E.activation(
                    esl, pbank[abk][:, (uc % 2) * TW:(uc % 2 + 1) * TW], AF.Identity,
                    scale=cgb[:, 32 + uc:33 + uc], bias=cgb[:, 40 + uc:41 + uc]),
                    waits=([t_acc, tok_mod_done] if uc == 0 else []) + [e1_free[uc % 2]])
                em.bankr(abk, t_e1)
                if taps:
                    t_tap = em.dma_group("tz", [(ac_s[uc * 128:(uc + 1) * 128, t0:t0 + TW], esl)],
                                         gate=[t_e1])
                    t_e1 = [t_e1, t_tap]
                t_s2 = em.op("vector", lambda E, uc=uc, esl=esl, hsl=hsl: E.scalar_tensor_tensor(
                    esl, hsl[:, uc * TW:(uc + 1) * TW], cgb[:, 32 + uc:33 + uc], esl,
                    ALU.mult, ALU.add), waits=[t_e1])
                t_out = em.op("vector", lambda E, uc=uc, esl=esl, xsl=xsl: E.tensor_tensor(
                    otile[:, (uc % 2) * TW:(uc % 2 + 1) * TW],
                    xsl[:, uc * TW:(uc + 1) * TW], esl, ALU.add),
                    waits=[ldx2, otile_gate[uc % 2]])
                em.bankr(abk, t_s2)
                e1_free[uc % 2] = t_out
                out_spill = em.dma_group("os", [
                    (outT[uc * 128:(uc + 1) * 128, t0:t0 + TW],
                     otile[:, (uc % 2) * TW:(uc % 2 + 1) * TW])] + ([
                    (xn_s[uc * 128:(uc + 1) * 128, t0:t0 + TW],
                     otile[:, (uc % 2) * TW:(uc % 2 + 1) * TW])] if taps else []),
                    gate=[t_out])
                otile_gate[uc % 2] = out_spill
            xm_rd2[buf] = t_out
            xw_free[buf] = t_out

        if phases <= 3:
            em.waits("sync", em.stream_tok("os") + em.stream_tok("xr") + em.stream_tok("x") +
                     [em.tcnt("tensor"), em.tcnt("vector"), em.tcnt("scalar")])
            em.finalize(ctx.enter_context(nc.Block()))
            return nc
        n_bc = [0]
        xcm = {"rowt": None, "cells24": None, "rowbc": None}
        em.op("vector", lambda E: E.memset(cells[0:1, 0:32], 0.0))
        em.op("vector", lambda E: E.memset(cells[0:1, 15:16], 1e-12), strict=True)
        t_rz = em.op("vector", lambda E: E.memset(rmax[:, 0:8], 0.0))

        def cell(i):
            return cells[0:1, i:i + 1]

        def cross_core_max(col_sl, k, ncell, cell_lo, ready):
            toks = []
            for j in range(ncell):
                i = n_bc[0]; n_bc[0] += 1
                tk1 = em.dma_group("bc", [(bc_d[i:i + 1, 0:128].flatten(), col_sl[:, j:j + 1])],
                                   gate=[ready])
                toks.append(em.dma_group("bc", [(rowt[0:1, j * 128:(j + 1) * 128],
                                                 bc_d[i:i + 1, 0:128])],
                                         gate=[tk1, xcm["rowt"]]))
            t_loc = None
            for j in range(ncell):
                t_loc = em.op("vector", lambda E, j=j: E.tensor_reduce(
                    cells[0:1, 24 + j:25 + j], rowt[0:1, j * 128:(j + 1) * 128], AX.X, ALU.max),
                    waits=[toks[j], xcm["cells24"]])
            if ncell == 1:
                t_loc = em.op("vector", lambda E: E.tensor_copy(cells[0:1, 25:26],
                                                                cells[0:1, 24:25]),
                              strict=True)
            xcm["rowt"] = t_loc
            tin = em.dma_group("cc", [(ar_in[k].ap(), cells[0:1, 24:26])], gate=[t_loc])
            xcm["cells24"] = tin
            em.cnt["ccdone"] += 1
            t_cc = ("ccdone", em.cnt["ccdone"])
            em.op("gpsimd", lambda E, k=k: E.collective_compute(
                "AllReduce", ALU.max, replica_groups=[list(range(NCORES))],
                ins=[ar_in[k].ap().opt()], outs=[ar_out[k].ap().opt()])
                .then_inc(em.sems["ccdone"], 1), waits=[tin], inc=False)
            return em.dma_group("cc", [(cells[0:1, cell_lo:cell_lo + ncell],
                                        ar_out[k].ap()[0:ncell])], gate=[t_cc])

        def bcast_col(cell_sl, col_dst, ready):
            i = n_bc[0]; n_bc[0] += 1
            t_r = em.op("vector", lambda E: E.tensor_scalar(rowt[0:1, 256:384], onesrow[0:1, :],
                                                            cell_sl, None, ALU.mult),
                        waits=[ready, xcm["rowbc"]], strict=True)
            t1 = em.dma_group("bc", [(bc_d[i:i + 1, 0:128], rowt[0:1, 256:384])], gate=[t_r])
            xcm["rowbc"] = t1
            return em.dma_group("bc", [(col_dst, bc_d[i:i + 1, 0:128].flatten())], gate=[t1])

        def s_mul(dst, a, b, waits=()):
            return em.op("scalar", lambda E: E.mul(dst, a, b), waits=waits, strict=True)

        def s_maxeps(dst, a, waits=()):
            return em.op("vector", lambda E: E.tensor_tensor(dst, a, cell(15), ALU.max),
                         waits=waits, strict=True)

        def s_recip(dst, a, waits=()):
            return em.op("vector", lambda E: E.reciprocal(dst, a), waits=waits, strict=True)

        t_h2_last = None
        t_rm = t_rz
        for w in range(NWIN):
            buf = w % 2
            t0 = w * TW
            xsl = xw[:, buf * DC * TW:(buf + 1) * DC * TW]
            ldo = em.dma_group("x", [
                (xsl[:, ic * TW:(ic + 1) * TW], outT[ic * 128:(ic + 1) * 128, t0:t0 + TW])
                for ic in range(DC)], gate=[xw_free[buf], em.stream_tok("os")])
            t_xbf = em.op("vector", lambda E, xsl=xsl: E.tensor_copy(xbf[:, :], xsl),
                          waits=[ldo, state["xbf_free"]])
            state["xbf_free"] = None
            st4, r4, mur4 = ln_stats(xbf[:, :], xsl, [t_xbf, ldo])
            state["xbf_free"] = state["sq_free"]
            t_n4 = ln_apply(xsl, r4, mur4, st4)
            t_h2_last = affine_out(
                lambda ic, t0=t0: h2_bf[:, ic * T + t0:ic * T + t0 + TW],
                lambda ic: cgb[:, 16 + ic:17 + ic], lambda ic: cgb[:, 24 + ic:25 + ic],
                t_n4, extra=[em.tcnt("tensor") if w == 0 else None])
            xw_free[buf] = [t_n4, t_h2_last]
            h3 = h2_bf[:, 0:DC * T].rearrange("p (c t) -> p c t", c=DC)[:, :, t0:t0 + TW]
            em.op("vector", lambda E, h3=h3: E.tensor_reduce(
                rmax[:, 1:2], h3, AX.XY, ALU.max, apply_absolute_value=True),
                waits=[t_h2_last, t_rm])
            t_rm = em.op("vector", lambda E: E.tensor_tensor(rmax[:, 0:1], rmax[:, 0:1],
                                                             rmax[:, 1:2], ALU.max))

        t_m1 = cross_core_max(rmax[:, 0:1], 0, 1, 0, t_rm)
        t_s = s_mul(cell(16), cell(0), 1.0 / QMAX, waits=[t_m1])
        t_s1 = s_maxeps(cell(1), cell(16), waits=[t_s])
        t_is1 = s_recip(cell(2), cell(1), waits=[t_s1])
        t_is1c = bcast_col(cell(2), colt[:, 0:1], t_is1)

        tok_xq = None
        first_q = [t_is1c, t_h2_last, em.stream_tok("os")]
        qt_free = [None, None]
        sq = 0
        for ic in range(DC):
            for half in range(T // SLAB):
                p = sq % 2
                sq += 1
                tmp = otile[:, 0:SLAB] if p == 0 else xw[:, 0:SLAB]
                sl = h2_bf[:, ic * T + half * SLAB:ic * T + (half + 1) * SLAB]
                isl = ipool[:, 0:SLAB]
                t_a = None
                for c in range(SLAB // 256):
                    t_a = em.op("scalar", lambda E, sl=sl, c=c, tmp=tmp: E.activation(
                        tmp[:, c * 256:(c + 1) * 256], sl[:, c * 256:(c + 1) * 256],
                        AF.Identity, scale=colt[:, 0:1]),
                        waits=(first_q + [qt_free[p]]) if c == 0 else [])
                if sq == 2:
                    first_q = []
                t_i = em.op("vector", lambda E, isl=isl, tmp=tmp: E.tensor_copy(
                    isl, tmp), waits=[t_a])
                qt_free[p] = t_i
                tok_xq = em.op("vector", lambda E, sl=sl, isl=isl: E.tensor_copy(sl, isl),
                               waits=[t_i])

        if phases <= 4:
            em.waits("vector", [tok_xq])
            em.finalize(ctx.enter_context(nc.Block()))
            return nc
        stg_free = [None, None, None, None]
        stgi = [0]
        t_rmug = None
        for im, (wsrc, dst_s, rm_col) in enumerate(((wuT, pu_s, 3), (wgT, pg_s, 4))):
            for ow in range(HF // NQ):
                wbuf = ow % 2
                wsl = ffnC[:, wbuf * DC * NQ:(wbuf + 1) * DC * NQ]
                ldw = em.dma_group("ff", [
                    (wsl[:, ic * NQ:(ic + 1) * NQ],
                     wsrc[ic * 128:(ic + 1) * 128, ow * NQ:(ow + 1) * NQ])
                    for ic in range(DC)], gate=[em.tcnt("tensor"), tok_w2a])
                for oc_rel in range(4):
                    oc = ow * 4 + oc_rel
                    for tb in range(T // NQ):
                        bk = 2 + (oc_rel * 8 + tb) % 2
                        sidx = stgi[0] % 4
                        for ic in range(DC):
                            em.op("tensor", lambda E, bk=bk, wsl=wsl, oc_rel=oc_rel, ic=ic, tb=tb:
                                  E.matmul(pbank[bk][:, 0:NQ],
                                           wsl[:, ic * NQ + oc_rel * 128:ic * NQ + oc_rel * 128 + 128],
                                           h2_bf[:, ic * T + tb * NQ:ic * T + (tb + 1) * NQ],
                                           start=(ic == 0), stop=(ic == DC - 1)),
                                  waits=em.bankw(bk, [ldw, tok_xq, stg_free[sidx]])
                                  if ic == 0 else [], inc=False)
                        t_mm = em.mark("tensor")
                        ssl = stg[:, sidx * NQ:(sidx + 1) * NQ]
                        t_red = em.op("vector", lambda E, bk=bk: E.tensor_reduce(
                            rmax[:, 2:3], pbank[bk][:, 0:NQ], AX.X, ALU.max,
                            apply_absolute_value=True), waits=[t_mm])
                        t_ev = em.op("vector", lambda E, bk=bk, ssl=ssl: E.tensor_copy(
                            ssl, pbank[bk][:, 0:NQ]))
                        em.bankr(bk, t_ev)
                        t_rmug = em.op("vector", lambda E, rm_col=rm_col: E.tensor_tensor(
                            rmax[:, rm_col:rm_col + 1], rmax[:, rm_col:rm_col + 1],
                            rmax[:, 2:3], ALU.max))
                        stg_free[sidx] = em.dma_group("sg", [
                            (dst_s[oc * 128:(oc + 1) * 128, tb * NQ:(tb + 1) * NQ], ssl)],
                            gate=[t_ev])
                        stgi[0] += 1
        tok_wd = em.dma_group("wd", wpairs(wd_bf, wdT, HFC, D), gate=[em.tcnt("tensor")])

        em.op("vector", lambda E: E.tensor_copy(rmax[:, 6:7], rmax[:, 3:4]), waits=[t_rmug])
        t_mm2 = em.op("vector", lambda E: E.tensor_copy(rmax[:, 7:8], rmax[:, 4:5]))
        t_ex2 = cross_core_max(rmax[:, 6:8], 1, 2, 3, t_mm2)
        t_a = s_mul(cell(16), cell(3), cell(1), waits=[t_ex2, t_s1])
        t_a = s_mul(cell(16), cell(16), 1.0 / QMAX, waits=[t_a])
        t_s2c = s_maxeps(cell(5), cell(16), waits=[t_a])
        t_b = s_mul(cell(17), cell(4), cell(1), waits=[t_ex2, t_s1])
        t_b = s_mul(cell(17), cell(17), 1.0 / QMAX, waits=[t_b])
        t_s3c = s_maxeps(cell(6), cell(17), waits=[t_b])
        t_r2 = s_recip(cell(18), cell(5), waits=[t_s2c])
        t_r3 = s_recip(cell(19), cell(6), waits=[t_s3c])
        t_cg = s_mul(cell(7), cell(1), cell(19), waits=[t_r3])
        t_cu = s_mul(cell(8), cell(1), cell(18), waits=[t_r2])
        tc_cg = bcast_col(cell(7), colt[:, 1:2], t_cg)
        tc_cu = bcast_col(cell(8), colt[:, 2:3], t_cu)
        tc_s3 = bcast_col(cell(6), colt[:, 3:4], t_s3c)

        if phases <= 5:
            em.waits("sync", em.stream_tok("sg"))
            em.waits("vector", [tc_s3])
            em.finalize(ctx.enter_context(nc.Block()))
            return nc
        in_free = [None, None, None, None]
        out_free = [None, None]
        fb_free = [None, None]
        fb2_free = [None, None]
        t4_store = [None] * (HFC * NSLAB)
        t_rm4 = None
        for s in range(HFC * NSLAB):
            oc, half = divmod(s, NSLAB)
            b = s % 2
            b4 = s % 4
            insl = ffnC[:, 8192 + b4 * SLAB:8192 + (b4 + 1) * SLAB]
            outsl = ffnC[:, 16384 + b * SLAB:16384 + (b + 1) * SLAB]
            ld = em.dma_group("pg", [
                (insl, pg_s[oc * 128:(oc + 1) * 128, half * SLAB:(half + 1) * SLAB])],
                gate=[in_free[b4]] + (em.stream_tok("sg") if s < 4 else []) +
                ([em.tcnt("tensor")] if s < 4 else []))
            i1 = ipool[:, 0:SLAB]
            f1 = xw[:, 0:SLAB] if b == 0 else xw[:, 3 * SLAB:4 * SLAB]
            f2 = xw[:, SLAB:2 * SLAB] if b == 0 else stat[:, 0:SLAB]
            f3 = xw[:, 2 * SLAB:3 * SLAB] if b == 0 else otile[:, 0:SLAB]
            t_a = None
            for c in range(SLAB // 256):
                t_a = em.op("scalar", lambda E, insl=insl, f1=f1, c=c: E.activation(
                    f1[:, c * 256:(c + 1) * 256], insl[:, c * 256:(c + 1) * 256],
                    AF.Identity, scale=colt[:, 1:2]),
                    waits=[ld, tc_cg, fb_free[b]] if c == 0 else [])
            in_free[b4] = t_a
            t_i1 = em.op("vector", lambda E, i1=i1, f1=f1: E.tensor_copy(i1, f1), waits=[t_a])
            t_f1 = em.op("vector", lambda E, i1=i1, f1=f1: E.tensor_copy(f1, i1))
            t_sg = None
            for c in range(max(1, SLAB // 512)):
                t_sg = em.op("scalar", lambda E, f1=f1, f2=f2, c=c: E.activation(
                    f2[:, c * 512:(c + 1) * 512], f1[:, c * 512:(c + 1) * 512],
                    AF.Sigmoid, scale=colt[:, 3:4]),
                    waits=[t_f1] + ([tc_s3, fb2_free[b]] if c == 0 else []))
            t_i2 = em.op("vector", lambda E, f2=f2, i1=i1: E.tensor_scalar(
                i1, f2, 65536.0, None, ALU.mult), waits=[t_sg])
            fb2_free[b] = t_i2
            t_t4 = em.op("vector", lambda E, i1=i1, f1=f1, f3=f3: E.scalar_tensor_tensor(
                f3, i1, 65535.0, f1, ALU.min, ALU.mult), waits=[t_i2])
            fb_free[b] = t_t4
            t_cv = em.op("vector", lambda E, f3=f3, outsl=outsl: E.tensor_copy(outsl, f3),
                         waits=[t_t4, out_free[b]])
            em.op("vector", lambda E, f3=f3: E.tensor_reduce(
                rmax[:, 2:3], f3, AX.X, ALU.max, apply_absolute_value=True))
            t_rm4 = em.op("vector", lambda E: E.tensor_tensor(rmax[:, 5:6], rmax[:, 5:6],
                                                              rmax[:, 2:3], ALU.max))
            st = em.dma_group("t4w", [
                (t4_s[oc * 128:(oc + 1) * 128, half * SLAB:(half + 1) * SLAB], outsl)],
                gate=[t_cv])
            out_free[b] = st
            t4_store[s] = st

        t_m4 = cross_core_max(rmax[:, 5:6], 2, 1, 9, t_rm4)
        t_c = s_mul(cell(16), cell(9), cell(6), waits=[t_m4, t_s3c])
        t_c = s_mul(cell(16), cell(16), 1.0 / (65536.0 * QMAX), waits=[t_c])
        t_s4c = s_maxeps(cell(10), cell(16), waits=[t_c])
        t_r4 = s_recip(cell(20), cell(10), waits=[t_s4c])
        t_c34 = s_mul(cell(11), cell(6), cell(20), waits=[t_r4])
        t_c34 = s_mul(cell(11), cell(11), 1.0 / 65536.0, waits=[t_c34])
        tc_c34 = bcast_col(cell(11), colt[:, 4:5], t_c34)

        if phases <= 6:
            em.waits("sync", em.stream_tok("t4w"))
            em.waits("vector", [tc_c34])
            em.finalize(ctx.enter_context(nc.Block()))
            return nc
        in2_free = [None, None, None, None]
        t5_store = [None] * (HFC * NSLAB)
        t_rm5 = None
        for s in range(HFC * NSLAB):
            oc, half = divmod(s, NSLAB)
            b = s % 2
            b4 = s % 4
            insl = ffnC[:, 8192 + b4 * SLAB:8192 + (b4 + 1) * SLAB]
            insl2 = ffnC[:, b4 * SLAB:(b4 + 1) * SLAB]
            outsl = ffnC[:, 16384 + b * SLAB:16384 + (b + 1) * SLAB]
            ld1 = em.dma_group("t4r", [
                (insl, t4_s[oc * 128:(oc + 1) * 128, half * SLAB:(half + 1) * SLAB])],
                gate=[in_free[b4], t4_store[s]])
            ld2 = em.dma_group("pur", [
                (insl2, pu_s[oc * 128:(oc + 1) * 128, half * SLAB:(half + 1) * SLAB])],
                gate=[in2_free[b4]] + ([em.tcnt("tensor")] if s < 4 else []))
            i1 = ipool[:, 0:SLAB]
            f1 = xw[:, 0:SLAB] if b == 0 else xw[:, 3 * SLAB:4 * SLAB]
            f2 = xw[:, SLAB:2 * SLAB] if b == 0 else stat[:, 0:SLAB]
            f3 = xw[:, 2 * SLAB:3 * SLAB] if b == 0 else otile[:, 0:SLAB]
            t_a = None
            for c in range(SLAB // 256):
                t_a = em.op("scalar", lambda E, insl=insl, f1=f1, c=c: E.activation(
                    f1[:, c * 256:(c + 1) * 256], insl[:, c * 256:(c + 1) * 256],
                    AF.Identity, scale=colt[:, 4:5]),
                    waits=[ld1, tc_c34, fb_free[b]] if c == 0 else [])
            in_free[b4] = t_a
            t_i3 = em.op("vector", lambda E, i1=i1, f1=f1: E.tensor_copy(i1, f1), waits=[t_a])
            t_f = em.op("vector", lambda E, i1=i1, f1=f1: E.tensor_copy(f1, i1))
            t_b = None
            for c in range(SLAB // 256):
                t_b = em.op("scalar", lambda E, insl2=insl2, f2=f2, c=c: E.activation(
                    f2[:, c * 256:(c + 1) * 256], insl2[:, c * 256:(c + 1) * 256],
                    AF.Identity, scale=colt[:, 2:3]),
                    waits=[ld2, tc_cu, fb2_free[b]] if c == 0 else [])
            in2_free[b4] = t_b
            t_i4 = em.op("vector", lambda E, i1=i1, f2=f2: E.tensor_copy(i1, f2),
                         waits=[t_b, t_f])
            fb2_free[b] = t_i4
            t_t5 = em.op("vector", lambda E, i1=i1, f1=f1, f3=f3: E.scalar_tensor_tensor(
                f3, i1, 1.0, f1, ALU.bypass, ALU.mult), waits=[t_i4])
            fb_free[b] = t_t5
            t_cv = em.op("vector", lambda E, f3=f3, outsl=outsl: E.tensor_copy(outsl, f3),
                         waits=[t_t5, out_free[b]])
            em.op("vector", lambda E, f3=f3: E.tensor_reduce(
                rmax[:, 2:3], f3, AX.X, ALU.max, apply_absolute_value=True))
            t_rm5 = em.op("vector", lambda E: E.tensor_tensor(rmax[:, 6:7], rmax[:, 6:7],
                                                              rmax[:, 2:3], ALU.max))
            st = em.dma_group("t5w", [
                (t5_s[oc * 128:(oc + 1) * 128, half * SLAB:(half + 1) * SLAB], outsl)],
                gate=[t_cv])
            out_free[b] = st
            t5_store[s] = st

        t_m5 = cross_core_max(rmax[:, 6:7], 3, 1, 12, t_rm5)
        t_d = s_mul(cell(16), cell(12), cell(10), waits=[t_m5, t_s4c])
        t_d = s_mul(cell(16), cell(16), cell(5), waits=[t_d, t_s2c])
        t_d = s_mul(cell(16), cell(16), 1.0 / QMAX, waits=[t_d])
        t_s5c = s_maxeps(cell(13), cell(16), waits=[t_d])
        t_r5 = s_recip(cell(21), cell(13), waits=[t_s5c])
        t_c45 = s_mul(cell(14), cell(10), cell(5), waits=[t_r5])
        t_c45 = s_mul(cell(14), cell(14), cell(21), waits=[t_c45])
        tc_c45 = bcast_col(cell(14), colt[:, 5:6], t_c45)
        tc_s5 = bcast_col(cell(13), colt[:, 6:7], t_s5c)
        t_gbs5 = em.op("vector", lambda E: E.tensor_tensor(
            cgb[:, 40:48], GBP, colt[:, 6:7].broadcast_to([128, 8]), ALU.mult),
            waits=[tc_s5, t_gab2], strict=True)
        if taps:
            em.dma_group("sg", [(dbg[0:1, 0:32], cells[0:1, 0:32])], gate=[t_gbs5])

        if phases <= 7:
            em.waits("sync", em.stream_tok("t5w"))
            em.waits("vector", [tc_c45])
            em.finalize(ctx.enter_context(nc.Block()))
            return nc
        oq_free = [None, None, None, None]
        d_mark = [None, None, None, None]
        st_free = [None, None]
        last_store = None
        prev_ep = None
        xn_gate = None
        for tb in range(T // NQ):
            ldxn = em.dma_group("x", [
                (xw[:, ic * NQ:(ic + 1) * NQ], outT[ic * 128:(ic + 1) * 128, tb * NQ:(tb + 1) * NQ])
                for ic in range(DC)],
                gate=[em.tcnt("vector"), em.tcnt("scalar"), xn_gate])
            t_dmm = None
            for ic in range(HFC):
                b = ic % 2
                b4 = ic % 4
                tsl = ffnC[:, b4 * NQ:(b4 + 1) * NQ]
                isl = ipool[:, 0:NQ]
                osl = ffnC[:, 8192 + b4 * NQ:8192 + (b4 + 1) * NQ]
                ld = em.dma_group("t5r", [
                    (tsl, t5_s[ic * 128:(ic + 1) * 128, tb * NQ:(tb + 1) * NQ])],
                    gate=[oq_free[b4]] + ([em.stream_tok("t5w"), em.tcnt("vector"),
                                          em.tcnt("scalar")]
                                         if (tb == 0 and ic < 4) else []))
                t_a = None
                for c in range(NQ // 256):
                    t_a = em.op("scalar", lambda E, tsl=tsl, c=c, b=b: E.activation(
                        stat[:, b * NQ + c * 256:b * NQ + (c + 1) * 256],
                        tsl[:, c * 256:(c + 1) * 256], AF.Identity, scale=colt[:, 5:6]),
                        waits=[ld, tc_c45, st_free[b]] if c == 0 else [])
                t_i = em.op("vector", lambda E, isl=isl, b=b: E.tensor_copy(
                    isl, stat[:, b * NQ:(b + 1) * NQ]), waits=[t_a])
                st_free[b] = t_i
                t_oq = em.op("vector", lambda E, isl=isl, osl=osl: E.tensor_copy(osl, isl),
                             waits=[t_i, d_mark[b4]])
                oq_free[b4] = t_oq
                for uc in range(DC):
                    em.op("tensor", lambda E, ic=ic, uc=uc, osl=osl: E.matmul(
                        pbank[uc][:, 0:NQ], wd_bf[:, ic * D + uc * 128:ic * D + (uc + 1) * 128],
                        osl, start=(ic == 0), stop=(ic == HFC - 1), skip_group_check=True),
                        waits=(sum((em.bankw(k, []) for k in range(8)), []) +
                               _flat([t_oq, tok_wd])) if (uc == 0 and ic == 0)
                        else ([t_oq] if uc == 0 else []),
                        inc=False)
                d_mark[b4] = em.mark("tensor")
            t_dmm = d_mark[(HFC - 1) % 4]
            for bk in range(8):
                em.bankclr(bk)
            for uc in range(DC):
                esl = e1t[:, (uc % 2) * NQ:(uc % 2 + 1) * NQ]
                t_e = em.op("scalar", lambda E, uc=uc, esl=esl: E.activation(
                    esl, pbank[uc][:, 0:NQ], AF.Identity, scale=cgb[:, 40 + uc:41 + uc]),
                    waits=([t_dmm, t_gbs5] if uc == 0 else []) + [e1_free[uc % 2]])
                em.bankr(uc, t_e)
                t_f = em.op("vector", lambda E, uc=uc, esl=esl: E.tensor_tensor(
                    otile[:, (uc % 2) * NQ:(uc % 2 + 1) * NQ],
                    xw[:, uc * NQ:(uc + 1) * NQ], esl, ALU.add),
                    waits=[t_e, ldxn, last_store])
                em.bankr(uc, t_f)
                e1_free[uc % 2] = t_f
                last_store = em.dma_group("fo", [
                    (outT[uc * 128:(uc + 1) * 128, tb * NQ:(tb + 1) * NQ],
                     otile[:, (uc % 2) * NQ:(uc % 2 + 1) * NQ])], gate=[t_f])
            xn_gate = last_store

        em.waits("sync", em.stream_tok("fo"))
        em.finalize(ctx.enter_context(nc.Block()))
    return nc


def _relayout(inputs):
    f = np.float32
    w = {}
    w["wqT"] = np.ascontiguousarray(np.asarray(inputs["L0_attn_wq"]).T.astype(f))
    w["wkT"] = np.ascontiguousarray(np.asarray(inputs["L0_attn_wk"]).T.astype(f))
    w["wvT"] = np.ascontiguousarray(np.asarray(inputs["L0_attn_wv"]).T.astype(f))
    w["w1T"] = np.ascontiguousarray(np.asarray(inputs["L0_attn_w1"]).T.astype(f))
    w["w2T"] = np.ascontiguousarray(np.asarray(inputs["L0_attn_w2"]).T.astype(f))
    w["wuT"] = np.ascontiguousarray(np.asarray(inputs["L0_ffn_w_up"]).T.astype(f))
    w["wgT"] = np.ascontiguousarray(np.asarray(inputs["L0_ffn_w_gate"]).T.astype(f))
    w["wdT"] = np.ascontiguousarray(np.asarray(inputs["L0_ffn_w_down"]).T.astype(f))
    w["wmod"] = np.ascontiguousarray(np.asarray(inputs["L0_wmod"]).astype(f))

    def colmajor(v, n):
        return np.ascontiguousarray(np.asarray(v).astype(f).reshape(n, 128).T)

    w["g1"] = colmajor(inputs["L0_attn_g1"], 8)
    w["be1"] = colmajor(inputs["L0_attn_be1"], 8)
    w["g2"] = colmajor(inputs["L0_attn_g2"], 8)
    w["be2"] = colmajor(inputs["L0_attn_be2"], 8)
    w["b2"] = colmajor(inputs["L0_attn_b2"], 8)
    w["b1"] = colmajor(inputs["L0_attn_b1"], 32)
    return w


def kernel(**inputs):
    shared = _relayout(inputs)
    x = np.asarray(inputs["L0_attn_x"]).astype(np.float32)
    c = np.asarray(inputs["L0_c"]).astype(np.float32)
    in_maps = []
    for i in range(NCORES):
        m = dict(shared)
        m["xT"] = np.ascontiguousarray(x[i].T)
        m["cc"] = np.ascontiguousarray(c[i].reshape(8, 128).T)
        in_maps.append(m)
    nc = build(taps=False)
    res = run_bass_kernel_spmd(nc, in_maps, core_ids=list(range(NCORES)))
    out = np.stack([np.asarray(r["outT"]).T for r in res.results], axis=0)
    return out.astype(np.float32)
